# version 11 (final)
# speedup vs baseline: 1.1814x; 1.0005x over previous
import numpy as np
import concourse.bass as bass
import concourse.mybir as mybir
from concourse.bass_utils import run_bass_kernel_spmd

F32 = mybir.dt.float32
BF16 = mybir.dt.bfloat16
I32 = mybir.dt.int32
AF = mybir.ActivationFunctionType
ALU = mybir.AluOpType
AX = mybir.AxisListType

NCORES = 8
D = 2048
DCH = 16
SEQ = 8192
T = SEQ // NCORES
NTILE = T // 128
G = 4096
GD = 256
H = 16
HD = 128
NCHUNK = T // 128
MLP = 16384
EPS_LN = 1e-5
EPS_FFN0 = float(np.finfo(np.float32).eps)
NEG_INF = -1e30
C_GELU = 1.5957691216057308


def _perm(n, block=256):
    idx = []
    for b in range(n // block):
        idx.append(np.arange(b * block, (b + 1) * block, 2))
    for b in range(n // block):
        idx.append(np.arange(b * block + 1, (b + 1) * block, 2))
    return np.concatenate(idx)

PERM_D = _perm(D)
PERM_N = _perm(MLP)
INV_PERM_D = np.argsort(PERM_D)



class Sem:
    __slots__ = ("h", "val")
    def __init__(self, handle):
        self.h = handle
        self.val = 0


class _Instr:
    __slots__ = ("ins", "inc_rec")
    def __init__(self, ins):
        self.ins = ins
        self.inc_rec = None
    def then_inc(self, sem, value=1):
        sem.val += value
        self.inc_rec = (sem, sem.val)
        if self.ins is not None:
            self.ins.then_inc(sem.h, value)
        return self


class EngProxy:
    def __init__(self, ctx, name):
        self._ctx = ctx
        self._name = name

    @property
    def _active(self):
        return self._ctx.active == self._name

    def wait(self, sem, val=None):
        v = sem.val if val is None else val
        if v <= 0:
            return
        if self._active:
            self._ctx.real_engines[self._name].wait_ge(sem.h, v)

    _AUTO_CHAIN = {"vector": "ve_s", "scalar": "ac_s", "gpsimd": "gp_s"}

    def __getattr__(self, attr):
        if attr.startswith("_"):
            raise AttributeError(attr)
        ch = self._AUTO_CHAIN.get(self._name)
        chs = self._ctx.sems[ch] if ch else None
        is_dma = attr in ("dma_start", "dma_start_transpose", "indirect_dma_start",
                          "collective_compute", "load_library", "nop")
        chained = (chs is not None) and not is_dma
        name = self._name
        ctx = self._ctx
        if self._active:
            real_fn = getattr(ctx.real_engines[name], attr)
        else:
            real_fn = None

        def call(*a, **k):
            if chained:
                prev = ctx.last_op.get(name)
                if prev is not None:
                    if prev.inc_rec is None:
                        prev.then_inc(chs, 1)
                    psem, pval = prev.inc_rec
                    if self._active:
                        ctx.real_engines[name].wait_ge(psem.h, pval)
            ins = _Instr(real_fn(*a, **k) if real_fn is not None else None)
            if chained:
                ctx.last_op[name] = ins
            return ins
        return call


class Ctx:
    def __init__(self, nc):
        self.nc = nc
        self.real_engines = {}
        self.active = None
        self.last_op = {}
        self.sems = {}
        self.q = EngProxy(self, "sync")
        self.t = EngProxy(self, "tensor")
        self.v = EngProxy(self, "vector")
        self.s = EngProxy(self, "scalar")
        self.g = EngProxy(self, "gpsimd")

    def sem(self, name):
        return self.sems[name]

    def run_bg(self, n):
        tasks = getattr(self, "bg_tasks", None)
        if not tasks:
            return
        for _ in range(min(n, len(tasks))):
            tasks.pop(0)()

    def barrier(self, exclude=("cc_a", "cc_b")):
        for name, chname in EngProxy._AUTO_CHAIN.items():
            prev = self.last_op.get(name)
            if prev is not None and prev.inc_rec is None:
                prev.then_inc(self.sems[chname], 1)
        self.last_op = {}
        for eng in (self.q, self.t, self.v, self.s, self.g):
            for n, s in self.sems.items():
                if n in exclude:
                    continue
                eng.wait(s, s.val)

    def vch(self, instr):
        return instr

    def sch(self, instr):
        return instr

    def reset_pass(self, active):
        self.active = active
        self.last_op = {}
        self.bg_tasks = list(getattr(self, "bg_factory", lambda: [])())
        for s in self.sems.values():
            s.val = 0


SEM_NAMES = [
    "dma_a", "dma_b", "dma_c", "dma_d", "dma_e", "st_d", "st_e",
    "st_o",
    "mm_a", "mm_b", "mm_c",
    "ve_a", "ve_b", "ve_c",
    "ac_a", "ac_b",
    "gp_a", "ve_s", "ac_s", "gp_s",
    "cc_a", "cc_b",
]


def build_program(nc, emit):
    from contextlib import ExitStack
    with ExitStack() as stack:
        handles = {n: stack.enter_context(nc.semaphore(n)) for n in SEM_NAMES}
        ctx = Ctx(nc)
        ctx.sems = {n: Sem(h) for n, h in handles.items()}
        with nc.Block() as block:
            def run(engine_name):
                def f(eng):
                    ctx.real_engines[engine_name] = eng
                    ctx.reset_pass(engine_name)
                    emit(ctx)
                return f
            block.sync(run("sync"))
            block.tensor(run("tensor"))
            block.vector(run("vector"))
            block.scalar(run("scalar"))
            block.gpsimd(run("gpsimd"))
    return nc


class Arena:
    def __init__(self, handle, words):
        self.h = handle
        self.words = words
        self.off = 0

    def carve(self, shape, dtype, align=64):
        self.off = (self.off + align - 1) // align * align
        esz = mybir.dt.size(dtype)
        free = 1
        for s in shape[1:]:
            free *= s
        nbytes = free * esz
        start_w = self.off // 4
        need_w = (nbytes + 3) // 4
        assert start_w + need_w <= self.words, \
            f"arena overflow: need {self.off + nbytes}, have {self.words * 4}"
        view = self.h.ap()[0:shape[0], start_w:start_w + need_w].bitcast(dtype)[:, :free]
        if len(shape) == 3:
            view = view.rearrange("p (a c) -> p a c", a=shape[1])
        self.off += nbytes
        return view

    def mark(self):
        return self.off

    def release(self, off):
        self.off = off



def host_prep(inputs):
    ac = np.ascontiguousarray
    pd = PERM_D
    pn = PERM_N
    p = {}
    p["xT"] = ac(inputs["L0_attn_x"][0].T[pd])
    p["Wq0"] = ac(inputs["L0_attn_Wq"][pd])
    p["Wk0"] = ac(inputs["L0_attn_Wk"][pd])
    p["Wv0"] = ac(inputs["L0_attn_Wv"][pd])
    p["Wo0"] = ac(inputs["L0_attn_Wo"][:, pd])
    p["Wg1"] = ac(inputs["L0_attn_Wg1"][pd])
    p["Wg2"] = ac(inputs["L0_attn_Wg2"])
    p["nw0"] = ac(inputs["L0_attn_norm_w"].reshape(DCH, 128).T)
    p["ld0"] = ac(inputs["L0_attn_log_decay"].reshape(1, H))
    p["WcT"] = ac(inputs["L0_ffn_Wc"].T[pd])
    p["genes"] = ac(inputs["L0_ffn_genes"])
    p["WdT"] = ac(inputs["L0_ffn_Wd"][pd].T)
    p["WuT"] = ac(inputs["L0_ffn_Wu"][pd].T)
    p["nwf"] = ac(inputs["L0_ffn_norm_w"][pd].reshape(DCH, 128).T)
    p["temp"] = ac(inputs["L0_ffn_temperature"].reshape(1, 1))
    p["fscale"] = ac(inputs["L0_ffn_scale"].reshape(1, 1))
    p["Wq1"] = ac(inputs["L1_attn_wq"][pd])
    p["Wk1"] = ac(inputs["L1_attn_wk"][pd])
    p["Wv1"] = ac(inputs["L1_attn_wv"][pd])
    p["Wg1a"] = ac(inputs["L1_attn_wg"][pd])
    p["Wo1"] = ac(inputs["L1_attn_wo"][:, pd])
    p["bq1"] = ac(inputs["L1_attn_bq"].reshape(DCH, 128).T)
    p["bk1"] = ac(inputs["L1_attn_bk"].reshape(DCH, 128).T)
    p["bv1"] = ac(inputs["L1_attn_bv"].reshape(DCH, 128).T)
    p["bg1"] = ac(inputs["L1_attn_bg"].reshape(DCH, 128).T)
    p["bo1"] = ac(inputs["L1_attn_bo"][pd].reshape(DCH, 128).T)
    p["bv1r"] = ac(inputs["L1_attn_bv"].reshape(1, D))
    p["bg1r"] = ac(inputs["L1_attn_bg"].reshape(1, D))
    p["gnw1"] = ac(inputs["L1_attn_gn_w"].reshape(1, D))
    p["gnb1"] = ac(inputs["L1_attn_gn_b"].reshape(1, D))
    p["gamma1"] = ac(inputs["L1_attn_gamma"].reshape(1, H))
    p["gWp"] = ac(inputs["L1_ffn_gate_W_packed"][pn].T)
    p["uWp"] = ac(inputs["L1_ffn_up_W_packed"][pn].T)
    p["dWp"] = ac(inputs["L1_ffn_down_W_packed"][pd].T)
    p["gSc"] = ac(inputs["L1_ffn_gate_scales"][pn].T)
    p["uSc"] = ac(inputs["L1_ffn_up_scales"][pn].T)
    p["dSc"] = ac(inputs["L1_ffn_down_scales"][pd].T)
    p["identity"] = np.eye(128, dtype=np.float32)
    return p




def _cm(dram_ap, p=128):
    return dram_ap.rearrange("(a p) n -> p a n", p=p)


def _sb3(sb_ap, a):
    return sb_ap.rearrange("p (a n) -> p a n", a=a)


def emit_base_init(ctx, b, hw, load_x=True):
    s_d = ctx.sem("dma_b")
    s_v = ctx.sem("ve_a")
    ctx.v.memset(b["ones_bf"][:, :], 1.0)
    ctx.v.memset(b["ones_row_f32"][:, :], 1.0)
    ctx.v.memset(b["ones_row_bf"][:, :], 1.0)
    ctx.v.memset(b["ones_col_f32"][:, :], 1.0)
    for i, (k, val) in enumerate(b["cst_vals"]):
        ctx.v.memset(b["cpool"][:, i:i + 1], val)
    ctx.q.dma_start(out=b["ident_f"][:, :], in_=hw["identity"][:, :]).then_inc(s_d, 16)
    if load_x:
        for qq in range(4):
            ctx.q.dma_start(out=_sb3(b["x"][:, qq * 4 * T:(qq + 1) * 4 * T], 4),
                            in_=_cm(hw["xT_in"][qq * 512:(qq + 1) * 512, :])).then_inc(s_d, 16)
    ctx.v.wait(s_d, s_d.val)
    ctx.v.tensor_copy(b["idb"][:, :], b["ident_f"][:, :]).then_inc(s_v, 1)
    for e in (ctx.q, ctx.t, ctx.v, ctx.s, ctx.g):
        e.wait(s_d, s_d.val)
        e.wait(s_v, s_v.val)


def alloc_base(ar, ps):
    b = {"T": T, "ps": ps}
    b["ones_bf"] = ar.carve([128, 1], BF16)
    b["ones_row_f32"] = ar.carve([1, 128], F32)
    b["ones_row_bf"] = ar.carve([1, 512], BF16)
    b["ones_col_f32"] = ar.carve([128, 1], F32)
    cst_vals = [("eps_ln", EPS_LN), ("eps_f", EPS_FFN0), ("mhalf", -0.5),
                ("invD", 1.0 / D), ("cgelu", C_GELU)]
    b["cpool"] = ar.carve([128, len(cst_vals) + 2], F32)
    b["cst_vals"] = cst_vals
    b["cst"] = {k: b["cpool"][:, i:i + 1] for i, (k, v) in enumerate(cst_vals)}
    b["ident_f"] = ar.carve([128, 128], F32)
    b["idb"] = ar.carve([128, 128], BF16)
    b["x"] = ar.carve([128, DCH * T], F32)
    return b



def emit_ln(ctx, b, x_chunks, xn_b, gw, g0, *, rms_fold=False, w_col=None,
            dst_stride=None, dst_off=0, a_s=None, b_s=None):
    s_ld = ctx.sem("ac_a")
    s_mm = ctx.sem("mm_a")
    s_dv = ctx.sem("ve_a")
    s_ac = ctx.sem("ac_b")
    s_ap = ctx.sem("ve_b")
    if dst_stride is None:
        dst_stride = gw
    xq = b["ln_xq"]
    ps_s1 = b["ps"][0][0:1, 0:gw]
    ps_s2 = b["ps"][1][0:1, 0:gw]
    st = b["ln_st"]
    ones_bf = b["ones_bf"]
    ones_row = b["ones_row_f32"]
    ps_a = b["ps"][2][:, 0:gw]
    ps_b = b["ps"][3][:, 0:gw]
    Dn = float(D)

    mm0 = s_mm.val
    ld0_ = s_ld.val
    for gs in ("ve_a", "ve_b", "ve_c", "ac_a", "ac_b"):
        ctx.t.wait(ctx.sem(gs), ctx.sem(gs).val)
    ones_col = b["ones_col_f32"]
    for c in range(DCH):
        pp = (c % 2) * gw
        if c >= 2:
            ctx.s.wait(s_mm, mm0 + 2 * (c - 1))
        src = x_chunks[c][:, g0:g0 + gw]
        ctx.s.activation(xq[:, pp:pp + gw], src, AF.Square, bias=0.0, scale=1.0).then_inc(s_ld, 2)
        ctx.t.wait(s_ld, ld0_ + 2 * (c + 1))
        ctx.t.matmul(ps_s1, ones_col[:, :], src, start=(c == 0), stop=(c == DCH - 1))
        ctx.t.matmul(ps_s2, ones_col[:, :], xq[:, pp:pp + gw], start=(c == 0), stop=(c == DCH - 1)).then_inc(s_mm, 2)

    m = st[:, 0:gw]; q = st[:, gw:2 * gw]; t1 = st[:, 2 * gw:3 * gw]
    v1 = st[:, 3 * gw:4 * gw]; r1 = st[:, 4 * gw:5 * gw]; b1 = st[:, 5 * gw:6 * gw]
    t2 = st[:, 6 * gw:7 * gw]; r2 = st[:, 7 * gw:8 * gw]
    ctx.v.wait(s_mm, s_mm.val)
    for gs in ("st_o", "ve_c", "ac_b", "ac_a"):
        ctx.v.wait(ctx.sem(gs), ctx.sem(gs).val)
    ctx.v.tensor_scalar(m, ps_s1, 1.0 / Dn, None, op0=ALU.mult)
    ctx.vch(ctx.v.tensor_scalar(q, ps_s2, 1.0 / Dn, None, op0=ALU.mult))
    ctx.vch(ctx.v.tensor_tensor(t1, m, m, op=ALU.mult))
    ctx.v.tensor_tensor(v1, q, t1, op=ALU.subtract).then_inc(s_dv, 1)
    ctx.s.wait(s_dv, s_dv.val)
    ctx.s.activation(t1, v1, AF.Ln, bias=b["cst"]["eps_ln"][0:1, :], scale=1.0).then_inc(s_ac, 1)
    ctx.s.wait(s_ac, s_ac.val)
    ctx.s.activation(r1, t1, AF.Exp, bias=0.0, scale=b["cst"]["mhalf"][0:1, :]).then_inc(s_ac, 1)
    ctx.v.wait(s_ac, s_ac.val)
    ctx.v.scalar_tensor_tensor(b1, m, -1.0, r1, op0=ALU.mult, op1=ALU.mult).then_inc(s_dv, 1)
    if rms_fold:
        ctx.vch(ctx.v.tensor_tensor(t2, r1, r1, op=ALU.mult))
        ctx.vch(ctx.v.tensor_tensor(t2, t2, ps_s2, op=ALU.mult))
        ctx.vch(ctx.v.tensor_tensor(v1, r1, b1, op=ALU.mult))
        ctx.vch(ctx.v.scalar_tensor_tensor(v1, v1, 2.0, ps_s1, op0=ALU.mult, op1=ALU.mult))
        ctx.vch(ctx.v.tensor_tensor(t2, t2, v1, op=ALU.add))
        ctx.vch(ctx.v.scalar_tensor_tensor(v1, b1, Dn, b1, op0=ALU.mult, op1=ALU.mult))
        ctx.v.tensor_tensor(t2, t2, v1, op=ALU.add).then_inc(s_dv, 1)
        ctx.s.wait(s_dv, s_dv.val)
        ctx.s.activation(v1, t2, AF.Ln, bias=b["cst"]["eps_f"][0:1, :], scale=b["cst"]["invD"][0:1, :]).then_inc(s_ac, 1)
        ctx.s.wait(s_ac, s_ac.val)
        ctx.s.activation(r2, v1, AF.Exp, bias=0.0, scale=b["cst"]["mhalf"][0:1, :]).then_inc(s_ac, 1)
        ctx.v.wait(s_ac, s_ac.val)
        ctx.v.tensor_tensor(r1, r1, r2, op=ALU.mult).then_inc(s_dv, 1)
        ctx.v.tensor_tensor(b1, b1, r2, op=ALU.mult).then_inc(s_dv, 1)
    ctx.t.wait(s_dv, s_dv.val)
    ctx.t.matmul(ps_a, ones_row[:, :], r1, start=True, stop=True)
    ctx.t.matmul(ps_b, ones_row[:, :], b1, start=True, stop=True).then_inc(s_mm, 1)
    ctx.s.wait(s_mm, s_mm.val)
    ctx.s.activation(a_s[:, 0:gw], ps_a, AF.Copy, bias=0.0, scale=1.0)
    ctx.s.activation(b_s[:, 0:gw], ps_b, AF.Copy, bias=0.0, scale=1.0).then_inc(s_ac, 1)
    ctx.v.wait(s_ac, s_ac.val)
    tmp = b["ln_tmp"]
    for c in range(DCH):
        src = x_chunks[c][:, g0:g0 + gw]
        dst = xn_b[:, c * dst_stride + dst_off: c * dst_stride + dst_off + gw]
        ctx.vch(ctx.v.tensor_tensor(tmp[:, 0:gw], src, a_s[:, 0:gw], op=ALU.mult))
        if w_col is not None:
            ctx.vch(ctx.v.tensor_tensor(tmp[:, 0:gw], tmp[:, 0:gw], b_s[:, 0:gw], op=ALU.add))
            ctx.v.tensor_scalar(dst, tmp[:, 0:gw], w_col[:, c:c + 1], None, op0=ALU.mult).then_inc(s_ap, 1)
        else:
            ctx.v.tensor_tensor(dst, tmp[:, 0:gw], b_s[:, 0:gw], op=ALU.add).then_inc(s_ap, 1)


def emit_ffn0(ctx, b, hw):
    s_wl = ctx.sem("dma_a")
    s_lg = ctx.sem("mm_b")
    s_ev = ctx.sem("ac_a")
    s_tk = ctx.sem("ve_c")
    s_tp = ctx.sem("mm_c")
    s_te = ctx.sem("ac_b")
    s_ap = ctx.sem("ve_b")
    xres = b["x"]
    xch = [xres[:, c * T:(c + 1) * T] for c in range(DCH)]
    xn_b = b["xn_b"]; lgb = b["lgb"]; curp = b["curp"]; ef = b["ef"]
    wcs = b["wcs"]; genes_b = b["genes_b"]; WdT_b = b["WdT_b"]; WuT_b = b["WuT_b"]
    exprT = b["exprT"]; d_t = b["d_t"]; u_t = b["u_t"]
    v32 = b["v32"]; v32f = b["v32f"]; sc32 = b["sc32"]; e32 = b["e32"]
    invt = b["invt"]; wsc = b["wsc"]; nwf_sb = b["nwf_sb"]
    a_s = b["a_s"]; b_s = b["b_s"]
    ps = b["ps"]; idb = b["idb"]; tmp = b["ln_tmp"]

    GW = 256
    for grp in range(4):
        ctx.run_bg(20)
        g0 = grp * GW
        emit_ln(ctx, b, xch, xn_b, GW, g0, rms_fold=True, w_col=nwf_sb,
                dst_stride=GW, dst_off=0, a_s=a_s, b_s=b_s)
        xn_ready = s_ap.val
        for gc in range(8):
            ctx.q.wait(s_lg, s_lg.val)
            ctx.q.dma_start(out=_sb3(wcs[:, :], DCH),
                            in_=_cm(hw["wcT_bf"][:, gc * 512:(gc + 1) * 512])).then_inc(s_wl, 16)
            ctx.t.wait(s_wl, s_wl.val)
            ctx.t.wait(s_ap, xn_ready)
            ctx.t.wait(s_ev, s_ev.val)
            ctx.t.wait(s_te, s_te.val)
            for ti in range(2):
                pbank = ps[4 + ti][:, 0:512]
                for c in range(DCH):
                    lhs = xn_b[:, c * GW + ti * 128: c * GW + (ti + 1) * 128]
                    mm = ctx.t.matmul(pbank, lhs, wcs[:, c * 512:(c + 1) * 512],
                                      start=(c == 0), stop=(c == DCH - 1))
                    if ti == 1 and c == DCH - 1:
                        mm.then_inc(s_lg, 1)
            ctx.s.wait(s_lg, s_lg.val)
            ctx.s.wait(s_tk, s_tk.val)
            for ti in range(2):
                ctx.s.activation(lgb[:, ti * 4096 + gc * 512: ti * 4096 + (gc + 1) * 512],
                                 ps[4 + ti][:, 0:512], AF.Copy, bias=0.0,
                                 scale=invt[:, 0:1]).then_inc(s_ev, 1)
        for ti in range(2):
            tile = grp * 2 + ti
            lg = lgb[:, ti * 4096:(ti + 1) * 4096]
            ctx.v.wait(s_ev, s_ev.val)
            ctx.vch(ctx.v.tensor_copy(curp[:, :], lg))
            for r in range(4):
                ctx.vch(ctx.v.max(out=v32[:, r * 8:(r + 1) * 8], in_=curp[:, :]))
                ctx.vch(ctx.v.match_replace(out=curp[:, :], in_to_replace=v32[:, r * 8:(r + 1) * 8],
                                            in_values=curp[:, :], imm_value=NEG_INF))
            ctx.v.tensor_copy(v32f[:, 0:32], v32[:, :])
            ctx.v.tensor_scalar(v32f[:, 32:33], v32[:, 0:1], -1.0, None, op0=ALU.mult)
            ctx.v.tensor_copy(v32f[:, 33:34], v32[:, 31:32]).then_inc(s_tk, 1)
            ctx.s.wait(s_tk, s_tk.val)
            ctx.s.activation(e32[:, :], v32f[:, 0:32], AF.Exp,
                             bias=v32f[:, 32:33], scale=1.0).then_inc(s_te, 1)
            ctx.v.wait(s_te, s_te.val)
            ctx.v.reduce_sum(sc32[:, 0:1], e32[:, :], axis=AX.X).then_inc(s_tk, 1)
            ctx.s.wait(s_tk, s_tk.val)
            ctx.s.activation(sc32[:, 1:2], sc32[:, 0:1], AF.Ln, bias=0.0, scale=1.0).then_inc(s_te, 1)
            ctx.v.wait(s_te, s_te.val)
            ctx.v.tensor_tensor(sc32[:, 2:3], v32f[:, 32:33], sc32[:, 1:2],
                                op=ALU.subtract).then_inc(s_tk, 1)
            ctx.s.wait(s_tk, s_tk.val)
            for hh in range(2):
                sl = slice(hh * 2048, (hh + 1) * 2048)
                ctx.s.activation(ef[:, :], lg[:, sl], AF.Exp, bias=sc32[:, 2:3],
                                 scale=1.0).then_inc(s_te, 1)
                ctx.v.wait(s_te, s_te.val)
                ctx.v.scalar_tensor_tensor(curp[:, sl], lg[:, sl], v32f[:, 33:34], ef[:, :],
                                           op0=ALU.is_ge, op1=ALU.mult).then_inc(s_tk, 1)
                ctx.s.wait(s_tk, s_tk.val)
            tp0 = s_tp.val
            te0 = s_te.val
            ctx.t.wait(s_tk, s_tk.val)
            ctx.t.wait(s_te, te0)
            for gcc in range(32):
                pbv = ps[6 + (gcc % 2)].bitcast(BF16)[:, 0:128]
                if gcc >= 2:
                    ctx.t.wait(s_te, te0 + gcc - 1)
                ctx.t.transpose(pbv, curp[:, gcc * 128:(gcc + 1) * 128], idb[:, :]).then_inc(s_tp, 1)
                ctx.s.wait(s_tp, tp0 + gcc + 1)
                ctx.s.activation(curp[:, gcc * 128:(gcc + 1) * 128], pbv, AF.Copy,
                                 bias=0.0, scale=1.0).then_inc(s_te, 1)
            ctx.t.wait(s_te, s_te.val)
            for gd_c in range(2):
                for gcc in range(32):
                    mm = ctx.t.matmul(ps[4][:, 0:128],
                                      genes_b[:, gcc * 256 + gd_c * 128: gcc * 256 + (gd_c + 1) * 128],
                                      curp[:, gcc * 128:(gcc + 1) * 128],
                                      start=(gcc == 0), stop=(gcc == 31))
                    if gcc == 31:
                        mm.then_inc(s_lg, 1)
                ctx.s.wait(s_lg, s_lg.val)
                ctx.s.activation(exprT[:, gd_c * 128:(gd_c + 1) * 128], ps[4][:, 0:128],
                                 AF.Copy, bias=0.0, scale=1.0).then_inc(s_ev, 1)
                ctx.t.wait(s_ev, s_ev.val)
            te_base = s_te.val
            ap_base = s_ap.val
            for c in range(DCH):
                rot = c % 2
                if c >= 2:
                    ctx.t.wait(s_ap, ap_base + c - 1)
                    ctx.t.wait(s_te, te_base + c - 1)
                for cc in range(2):
                    ctx.t.matmul(ps[4 + rot][:, 0:128],
                                 WdT_b[:, cc * 2048 + c * 128: cc * 2048 + (c + 1) * 128],
                                 exprT[:, cc * 128:(cc + 1) * 128],
                                 start=(cc == 0), stop=(cc == 1))
                for cc in range(2):
                    mm = ctx.t.matmul(ps[6 + rot][:, 0:128],
                                      WuT_b[:, cc * 2048 + c * 128: cc * 2048 + (c + 1) * 128],
                                      exprT[:, cc * 128:(cc + 1) * 128],
                                      start=(cc == 0), stop=(cc == 1))
                    if cc == 1:
                        mm.then_inc(s_lg, 1)
                ctx.s.wait(s_lg, s_lg.val)
                ctx.s.activation(d_t[:, rot * 128:(rot + 1) * 128], ps[4 + rot][:, 0:128],
                                 AF.Tanh, bias=0.0, scale=1.0)
                ctx.s.activation(u_t[:, rot * 128:(rot + 1) * 128], ps[6 + rot][:, 0:128],
                                 AF.Copy, bias=0.0, scale=1.0).then_inc(s_te, 1)
                ctx.v.wait(s_te, s_te.val)
                tcol = tile * 128
                xsl = xch[c][:, tcol:tcol + 128]
                asl = a_s[:, ti * 128:(ti + 1) * 128]
                bsl = b_s[:, ti * 128:(ti + 1) * 128]
                ctx.v.tensor_tensor(tmp[:, 128:256], d_t[:, rot * 128:(rot + 1) * 128],
                                    u_t[:, rot * 128:(rot + 1) * 128], op=ALU.mult)
                ctx.vch(ctx.v.tensor_tensor(tmp[:, 0:128], xsl, asl, op=ALU.mult))
                ctx.vch(ctx.v.tensor_tensor(tmp[:, 0:128], tmp[:, 0:128], bsl, op=ALU.add))
                ctx.vch(ctx.v.tensor_scalar(tmp[:, 0:128], tmp[:, 0:128], nwf_sb[:, c:c + 1], None, op0=ALU.mult))
                ctx.vch(ctx.v.tensor_tensor(tmp[:, 0:128], tmp[:, 0:128], tmp[:, 128:256], op=ALU.mult))
                ctx.v.scalar_tensor_tensor(xsl, tmp[:, 0:128], wsc[:, 0:1], xsl,
                                           op0=ALU.mult, op1=ALU.add).then_inc(s_ap, 1)


def emit_ffn0_prologue(ctx, b, hw, first_inv):
    s_d = ctx.sem("dma_b")
    s_v = ctx.sem("ve_a")
    s_a = ctx.sem("ac_a")
    s_st = ctx.sem("dma_c")
    stage = b["stage"]
    stage_bf = b["stage_bf"]

    ctx.q.wait(ctx.sem("ve_b"), ctx.sem("ve_b").val)
    ctx.q.dma_start(out=b["nwf_sb"][:, :], in_=hw["nwf"][:, :]).then_inc(s_d, 16)
    ctx.q.dma_start(out=b["invt"][:, :], in_=hw["temp"][0:1, 0:1].to_broadcast([128, 1])).then_inc(s_d, 16)
    ctx.q.dma_start(out=b["wsc"][:, :], in_=hw["fscale"][0:1, 0:1].to_broadcast([128, 1])).then_inc(s_d, 16)
    ctx.v.wait(s_d, s_d.val)
    ctx.vch(ctx.v.tensor_scalar(b["invt"][:, :], b["invt"][:, :], 0.1, None, op0=ALU.max))
    ctx.vch(ctx.v.reciprocal(b["invt"][:, :], b["invt"][:, :]))
    ctx.v.tensor_scalar(b["wsc"][:, :], b["wsc"][:, :], 0.5, None, op0=ALU.mult).then_inc(s_v, 1)

    s_sg = ctx.sem("dma_a")
    if first_inv:
        for r in range(DCH):
            for hh in range(2):
                ctx.q.wait(s_st, s_st.val)
                ctx.q.dma_start(out=stage[:, :],
                                in_=hw["WcT"][r * 128:(r + 1) * 128, hh * 2048:(hh + 1) * 2048]).then_inc(s_sg, 16)
                ctx.v.wait(s_sg, s_sg.val)
                ctx.v.tensor_copy(stage_bf[:, :], stage[:, :]).then_inc(s_v, 1)
                ctx.g.wait(s_v, s_v.val)
                ctx.g.dma_start(out=hw["wcT_bf"][r * 128:(r + 1) * 128, hh * 2048:(hh + 1) * 2048],
                                in_=stage_bf[:, :]).then_inc(s_st, 16)
    for gcc in range(16):
        ctx.q.wait(s_v, s_v.val)
        ctx.q.dma_start(out=_sb3(stage[:, 0:512], 2),
                        in_=_cm(hw["genes"][gcc * 256:(gcc + 1) * 256, :])).then_inc(s_sg, 16)
        ctx.v.wait(s_sg, s_sg.val)
        ctx.v.tensor_copy(b["genes_b"][:, gcc * 512:(gcc + 1) * 512], stage[:, 0:512]).then_inc(s_v, 1)
    for (srcw, dst) in ((hw["WdT"], b["WdT_b"]), (hw["WuT"], b["WuT_b"])):
        for cc in range(2):
            ctx.q.wait(s_v, s_v.val)
            ctx.q.dma_start(out=stage[:, :], in_=srcw[cc * 128:(cc + 1) * 128, :]).then_inc(s_sg, 16)
            ctx.v.wait(s_sg, s_sg.val)
            ctx.v.tensor_copy(dst[:, cc * 2048:(cc + 1) * 2048], stage[:, :]).then_inc(s_v, 1)
    ctx.q.wait(s_st, s_st.val)
    ctx.q.wait(s_v, s_v.val)
    ctx.t.wait(s_v, s_v.val)
    ctx.s.wait(s_v, s_v.val)
    ctx.v.wait(s_v, s_v.val)
    ctx.g.wait(s_v, s_v.val)


def alloc_ffn0_bufs(ar, ps, base):
    b = dict(base)
    GW = 256
    b["nwf_sb"] = ar.carve([128, 16], F32)
    b["invt"] = ar.carve([128, 1], F32)
    b["wsc"] = ar.carve([128, 1], F32)
    b["v32"] = ar.carve([128, 32], BF16)
    b["v32f"] = ar.carve([128, 34], F32)
    b["sc32"] = ar.carve([128, 4], F32)
    b["e32"] = ar.carve([128, 32], F32)
    m = ar.mark()
    b["stage"] = ar.carve([128, 2048], F32)
    b["stage_bf"] = ar.carve([128, 2048], BF16)
    pro_end = ar.mark()
    ar.release(m)
    b["xn_b"] = ar.carve([128, DCH * GW], BF16)
    b["lgb"] = ar.carve([128, 2 * 4096], BF16)
    b["curp"] = ar.carve([128, 4096], BF16)
    b["ef"] = ar.carve([128, 2048], F32)
    b["wcs"] = ar.carve([128, DCH * 512], BF16)
    b["genes_b"] = ar.carve([128, 32 * 256], BF16)
    b["WdT_b"] = ar.carve([128, 2 * 2048], BF16)
    b["WuT_b"] = ar.carve([128, 2 * 2048], BF16)
    b["exprT"] = ar.carve([128, 256], BF16)
    b["d_t"] = ar.carve([128, 256], BF16)
    b["u_t"] = ar.carve([128, 256], BF16)
    b["a_s"] = ar.carve([128, GW], F32)
    b["b_s"] = ar.carve([128, GW], F32)
    b["ln_xq"] = ar.carve([128, 2 * GW], F32)
    b["ln_st"] = ar.carve([1, 8 * GW], F32)
    b["ln_tmp"] = ar.carve([128, GW], F32)
    assert ar.mark() >= pro_end or True
    return b


def build_s1(tokens=T):
    nc = bass.Bass("TRN2", target_bir_lowering=False, debug=False)
    hw = {}
    hw["xT_in"] = nc.dram_tensor("xT_in", [D, tokens], F32, kind="ExternalInput").ap()
    hw["WcT"] = nc.dram_tensor("WcT", [D, G], F32, kind="ExternalInput").ap()
    hw["genes"] = nc.dram_tensor("genes", [G, GD], F32, kind="ExternalInput").ap()
    hw["WdT"] = nc.dram_tensor("WdT", [GD, D], F32, kind="ExternalInput").ap()
    hw["WuT"] = nc.dram_tensor("WuT", [GD, D], F32, kind="ExternalInput").ap()
    hw["nwf"] = nc.dram_tensor("nwf", [128, DCH], F32, kind="ExternalInput").ap()
    hw["temp"] = nc.dram_tensor("temp", [1, 1], F32, kind="ExternalInput").ap()
    hw["fscale"] = nc.dram_tensor("fscale", [1, 1], F32, kind="ExternalInput").ap()
    hw["identity"] = nc.dram_tensor("identity", [128, 128], F32, kind="ExternalInput").ap()
    hw["xT_out"] = nc.dram_tensor("xT_out", [D, tokens], F32, kind="ExternalOutput").ap()
    hw["wcT_bf"] = nc.dram_tensor("wcT_bf", [D, G], BF16, kind="Internal").ap()

    from contextlib import ExitStack
    with ExitStack() as stack:
        arh = stack.enter_context(nc.sbuf_tensor("arena", [128, 48000], I32))
        ps = [stack.enter_context(nc.psum_tensor("ps%d" % i, [128, 512], F32)) for i in range(8)]
        ps = [p.ap() for p in ps]
        ar = Arena(arh, 48000)
        base = alloc_base(ar, ps)
        b = alloc_ffn0_bufs(ar, ps, base)

        def emit(ctx):
            s_d = ctx.sem("dma_b")
            s_o = ctx.sem("st_o")
            s_ap = ctx.sem("ve_b")
            emit_base_init(ctx, b, hw)
            emit_ffn0_prologue(ctx, b, hw, True)
            emit_ffn0(ctx, b, hw)
            ctx.g.wait(s_ap, s_ap.val)
            for qq in range(4):
                ctx.g.dma_start(out=_cm(hw["xT_out"][qq * 512:(qq + 1) * 512, :]),
                                in_=_sb3(b["x"][:, qq * 4 * T:(qq + 1) * 4 * T], 4)).then_inc(s_o, 16)
            ctx.g.wait(s_o, 64)

        build_program(nc, emit)
    return nc


TOKW = 256
NPASS = T // TOKW


def alloc_attn_bufs(ar, ps, base, *, recur=True):
    b = dict(base)
    b["o_sb"] = ar.carve([128, D], F32)
    b["ld_b"] = ar.carve([128, H], F32)
    b["ldx"] = ar.carve([128, H], F32)
    b["lam128"] = ar.carve([128, H], F32)
    b["kdec"] = ar.carve([128, H], F32)
    if recur:
        b["iota_d"] = b["o_sb"][:, 0:128]
        b["iota_r"] = b["o_sb"][:, 128:256]
        b["msk"] = b["o_sb"][:, 256:384]
    else:
        b["iota_d"] = ar.carve([128, 128], F32)
        b["iota_r"] = ar.carve([128, 128], F32)
    b["iota_c"] = ar.carve([128, 1], F32)
    if recur:
        b["DT"] = ar.carve([128, H * 128], BF16)
        b["qdecb"] = ar.carve([128, H * 128], BF16)
    b["a_s"] = ar.carve([128, TOKW], F32)
    b["b_s"] = ar.carve([128, TOKW], F32)
    b["ln_xq"] = ar.carve([128, 2 * TOKW], F32)
    b["ln_st"] = b["o_sb"][0:1, 0:8 * TOKW]
    b["ln_tmp"] = ar.carve([128, 2 * TOKW], F32)
    b["xn_b"] = ar.carve([128, DCH * TOKW], BF16)
    if recur:
        b["qT"] = ar.carve([128, H * TOKW], BF16)
        b["kT"] = ar.carve([128, H * TOKW], BF16)
        b["vtm"] = ar.carve([128, (TOKW // 128) * D], BF16)
    b["S"] = ar.carve([128, H * 128], F32)
    b["S_bf"] = ar.carve([128, H * 128], BF16)
    if recur:
        b["kp"] = ar.carve([128, D], BF16)
        b["qp"] = ar.carve([128, 256], BF16)
    b["wsl"] = ar.carve([128, DCH * 128], F32)
    b["wslb"] = ar.carve([128, DCH * 128], BF16)
    if recur:
        b["sT"] = ar.carve([128, 256], BF16)
        b["gate_tm"] = ar.carve([128, (TOKW // 128) * D], BF16)
    b["bq_sb"] = ar.carve([128, H], F32)
    b["bk_sb"] = ar.carve([128, H], F32)
    b["bv_row_b"] = ar.carve([1, D], BF16)
    b["bg_row_b"] = ar.carve([1, D], BF16)
    b["brow_f"] = ar.carve([1, D], F32)
    return b


def emit_attn_tables(ctx, b, hw, cfg):
    s_d = ctx.sem("dma_b")
    s_v = ctx.sem("ve_a")
    s_a = ctx.sem("ac_b")
    ctx.q.wait(ctx.sem("ve_b"), ctx.sem("ve_b").val)
    ctx.q.dma_start(out=b["ld_b"][:, :], in_=hw["ldsrc"][0:1, :].to_broadcast([128, H])).then_inc(s_d, 16)
    ctx.v.wait(s_d, s_d.val)
    ctx.v.tensor_copy(b["ldx"][:, :], b["ld_b"][:, :]).then_inc(s_v, 1)
    if cfg.get("ln_gamma"):
        ctx.s.wait(s_v, s_v.val)
        ctx.s.activation(b["ld_b"][:, :], b["ld_b"][:, :], AF.Ln, bias=0.0, scale=1.0).then_inc(s_a, 1)
        ctx.v.wait(s_a, s_a.val)
    ctx.g.iota(b["iota_d"][:, :], pattern=[[1, 128]], base=0, channel_multiplier=-1,
               allow_small_or_imprecise_dtypes=True)
    ctx.g.iota(b["iota_r"][:, :], pattern=[[1, 128]], base=1, channel_multiplier=0,
               allow_small_or_imprecise_dtypes=True)
    ctx.g.iota(b["iota_c"][:, :], pattern=[[1, 1]], base=127, channel_multiplier=-1,
               allow_small_or_imprecise_dtypes=True).then_inc(ctx.sem("ve_b"), 1)
    ctx.v.wait(ctx.sem("ve_b"), ctx.sem("ve_b").val)
    if not cfg.get("finish"):
        ctx.v.tensor_scalar(b["msk"][:, :], b["iota_d"][:, :], 0.0, None, op0=ALU.is_ge).then_inc(s_v, 1)
        ctx.v.tensor_scalar(b["iota_d"][:, :], b["iota_d"][:, :], 0.0, None, op0=ALU.max).then_inc(s_v, 1)
        ctx.v.tensor_scalar(b["ldx"][:, :], b["ld_b"][:, :], 128.0, None, op0=ALU.mult).then_inc(s_v, 1)
        ctx.s.wait(s_v, s_v.val)
        ctx.s.activation(b["lam128"][:, :], b["ldx"][:, :], AF.Exp, bias=0.0, scale=1.0)
    if not cfg.get("finish"):
        for h in range(H):
            ctx.s.activation(b["kdec"][:, h:h + 1], b["iota_c"][:, :], AF.Exp,
                             bias=0.0, scale=b["ld_b"][:, h:h + 1])
        tmp = b["ln_tmp"]
        for h in range(H):
            ctx.s.activation(tmp[:, 0:128], b["iota_d"][:, :], AF.Exp,
                             bias=0.0, scale=b["ld_b"][:, h:h + 1]).then_inc(s_a, 1)
            ctx.v.wait(s_a, s_a.val)
            ctx.v.tensor_tensor(b["DT"][:, h * 128:(h + 1) * 128], tmp[:, 0:128],
                                b["msk"][:, :], op=ALU.mult).then_inc(s_v, 1)
            ctx.s.wait(s_v, s_v.val)
            ctx.s.activation(b["qdecb"][:, h * 128:(h + 1) * 128], b["iota_r"][:, :], AF.Exp,
                             bias=0.0, scale=b["ld_b"][:, h:h + 1]).then_inc(s_a, 1)
    if cfg.get("biases"):
        ctx.q.wait(s_d, s_d.val)
        ctx.q.dma_start(out=b["bq_sb"][:, :], in_=hw["bq"][:, :]).then_inc(s_d, 16)
        ctx.q.dma_start(out=b["bk_sb"][:, :], in_=hw["bk"][:, :]).then_inc(s_d, 16)
        for key, dst in (("bv_row", b["bv_row_b"]), ("bg_row", b["bg_row_b"])):
            if key == "bg_row" and not cfg.get("gate_full"):
                continue
            ctx.q.wait(ctx.sem("ve_a"), ctx.sem("ve_a").val)
            ctx.q.dma_start(out=b["brow_f"][:, :], in_=hw[key][:, :]).then_inc(s_d, 16)
            ctx.v.wait(s_d, s_d.val)
            ctx.v.tensor_copy(dst[0:1, :], b["brow_f"][0:1, :]).then_inc(s_v, 1)
    ctx.v.memset(b["S"][:, :], 0.0)
    ctx.v.memset(b["S_bf"][:, :], 0.0).then_inc(s_v, 1)
    ctx.t.wait(s_v, s_v.val)
    ctx.t.wait(s_a, s_a.val)
    ctx.g.wait(s_v, s_v.val)
    ctx.q.wait(s_v, s_v.val)


def emit_attn_proj(ctx, b, hw, cfg, pi):
    s_wd = ctx.sem("dma_a")
    s_cv = ctx.sem("ve_a")
    s_mm = ctx.sem("mm_a")
    s_ev = ctx.sem("ac_a")
    s_ap = ctx.sem("ve_b")
    xch = [b["x"][:, c * T:(c + 1) * T] for c in range(DCH)]
    emit_ln(ctx, b, xch, b["xn_b"], TOKW, pi * TOKW, rms_fold=False, w_col=None,
            dst_stride=TOKW, dst_off=0, a_s=b["a_s"], b_s=b["b_s"])
    xn_ready = s_ap.val
    silu = cfg.get("silu", False)
    has_bias = cfg.get("biases", False)
    wsl = b["wsl"]; wslb = b["wslb"]; xn = b["xn_b"]
    ctx.s.wait(ctx.sem("ve_c"), ctx.sem("ve_c").val)
    for wkey, dst, bias_key in (("Wq", b["qT"], "bq"), ("Wk", b["kT"], "bk")):
        for h in range(H):
            ctx.q.wait(s_cv, s_cv.val)
            ctx.q.dma_start(out=_sb3(wsl[:, 0:D], DCH),
                            in_=_cm(hw[wkey][:, h * 128:(h + 1) * 128])).then_inc(s_wd, 16)
            ctx.v.wait(s_wd, s_wd.val)
            ctx.v.wait(s_mm, s_mm.val)
            ctx.v.tensor_copy(wslb[:, 0:D], wsl[:, 0:D]).then_inc(s_cv, 1)
            ctx.t.wait(s_cv, s_cv.val)
            ctx.t.wait(s_ap, xn_ready)
            ctx.t.wait(s_ev, s_ev.val)
            pb = b["ps"][h % 2][:, 0:TOKW]
            for c in range(DCH):
                mm = ctx.t.matmul(pb, wslb[:, c * 128:(c + 1) * 128],
                                  xn[:, c * TOKW:(c + 1) * TOKW],
                                  start=(c == 0), stop=(c == DCH - 1))
                if c == DCH - 1:
                    mm.then_inc(s_mm, 1)
            ctx.s.wait(s_mm, s_mm.val)
            if silu:
                ctx.s.activation(dst[:, h * TOKW:(h + 1) * TOKW], pb, AF.Silu,
                                 bias=0.0, scale=1.0).then_inc(s_ev, 1)
            else:
                ctx.s.activation(dst[:, h * TOKW:(h + 1) * TOKW], pb, AF.Identity,
                                 bias=b[bias_key + "_sb"][:, h:h + 1], scale=1.0).then_inc(s_ev, 1)
    tm_list = [("Wv", b["vtm"], "bv", AF.Silu if silu else AF.Copy)]
    if cfg.get("gate_full"):
        tm_list.append(("Wg1a", b["gate_tm"], "bg", AF.Silu))
    ntt = TOKW // 128
    for wkey, dst, bias_key, func in tm_list:
        for eg in range(4):
            mm_base = s_mm.val
            for c in range(DCH):
                slot = (c % 2) * 512
                ctx.q.wait(s_cv, s_cv.val)
                ctx.q.dma_start(out=wsl[:, slot:slot + 512],
                                in_=hw[wkey][c * 128:(c + 1) * 128, eg * 512:(eg + 1) * 512]).then_inc(s_wd, 16)
                ctx.v.wait(s_wd, s_wd.val)
                if c >= 2:
                    ctx.v.wait(s_mm, mm_base + c - 1)
                else:
                    ctx.v.wait(s_mm, mm_base)
                ctx.v.tensor_copy(wslb[:, slot:slot + 512], wsl[:, slot:slot + 512]).then_inc(s_cv, 1)
                ctx.t.wait(s_cv, s_cv.val)
                if c == 0:
                    ctx.t.wait(s_ev, s_ev.val)
                for tt in range(ntt):
                    pb = b["ps"][2 + tt][:, 0:512]
                    is_last = (c == DCH - 1) and not has_bias
                    mm = ctx.t.matmul(pb, xn[:, c * TOKW + tt * 128: c * TOKW + (tt + 1) * 128],
                                      wslb[:, slot:slot + 512], start=(c == 0), stop=is_last)
                    if tt == ntt - 1:
                        mm.then_inc(s_mm, 1)
            if has_bias:
                brow = b[bias_key + "_row_b"]
                for tt in range(ntt):
                    pb = b["ps"][2 + tt][:, 0:512]
                    mm = ctx.t.matmul(pb, b["ones_row_bf"][0:1, 0:128],
                                      brow[0:1, eg * 512:(eg + 1) * 512],
                                      start=False, stop=True)
                    if tt == ntt - 1:
                        mm.then_inc(s_mm, 1)
            ctx.s.wait(s_mm, s_mm.val)
            for tt in range(ntt):
                pb = b["ps"][2 + tt][:, 0:512]
                ctx.s.activation(dst[:, tt * D + eg * 512: tt * D + (eg + 1) * 512], pb,
                                 func, bias=0.0, scale=1.0).then_inc(s_ev, 1)


def emit_attn_chunks(ctx, b, cfg, pi, o_out_fn):
    s_tp = ctx.sem("mm_c")
    s_te = ctx.sem("ac_b")
    s_mh = ctx.sem("mm_a")
    s_sd = ctx.sem("ve_c")
    s_oe = ctx.sem("ac_a")
    ps = b["ps"]; idb = b["idb"]
    for cc in range(TOKW // 128):
        tp0 = s_tp.val
        te0 = s_te.val
        ctx.t.wait(s_sd, s_sd.val)
        ctx.t.wait(s_te, te0)
        for h in range(H):
            pbv = ps[6 + (h % 2)].bitcast(BF16)[:, 0:128]
            if h >= 2:
                ctx.t.wait(s_te, te0 + h - 1)
            ctx.t.transpose(pbv, b["kT"][:, h * TOKW + cc * 128: h * TOKW + (cc + 1) * 128],
                            idb[:, :]).then_inc(s_tp, 1)
            ctx.v.wait(s_tp, tp0 + h + 1)
            ctx.v.tensor_scalar(b["kp"][:, h * 128:(h + 1) * 128], pbv,
                                b["kdec"][:, h:h + 1], None, op0=ALU.mult).then_inc(s_te, 1)
        oe0 = s_oe.val
        sd0 = s_sd.val
        mh0 = s_mh.val
        ctx.t.wait(s_te, s_te.val)
        ctx.v.wait(s_te, s_te.val)
        for h in range(H):
            qslot = (h % 2) * 128
            sc_bank = ps[h % 2]
            o_bank = ps[2 + h // 4]
            o_off = (h % 4) * 128
            su_bank = ps[6 + h % 2]
            kTs = b["kT"][:, h * TOKW + cc * 128: h * TOKW + (cc + 1) * 128]
            qTs = b["qT"][:, h * TOKW + cc * 128: h * TOKW + (cc + 1) * 128]
            vs = b["vtm"][:, cc * D + h * 128: cc * D + (h + 1) * 128]
            if h >= 2:
                ctx.v.wait(s_mh, mh0 + 3 * (h - 2) + 2)
            ctx.v.tensor_tensor(b["qp"][:, qslot:qslot + 128], qTs,
                                b["qdecb"][:, h * 128:(h + 1) * 128], op=ALU.mult).then_inc(s_sd, 1)
            if h >= 2:
                ctx.t.wait(s_sd, sd0 + 3 * (h - 2) + 2)
            if h % 4 == 0:
                ctx.t.wait(s_oe, oe0)
            ctx.t.matmul(sc_bank[:, 0:128], kTs, qTs, start=True, stop=True).then_inc(s_mh, 1)
            ctx.v.wait(s_mh, mh0 + 3 * h + 1)
            ctx.v.tensor_tensor(b["sT"][:, qslot:qslot + 128], sc_bank[:, 0:128],
                                b["DT"][:, h * 128:(h + 1) * 128], op=ALU.mult).then_inc(s_sd, 1)
            ctx.t.wait(s_sd, sd0 + 3 * h + 2)
            ctx.t.matmul(o_bank[:, o_off:o_off + 128], b["sT"][:, qslot:qslot + 128], vs,
                         start=True, stop=False)
            ctx.t.matmul(o_bank[:, o_off:o_off + 128], b["qp"][:, qslot:qslot + 128],
                         b["S_bf"][:, h * 128:(h + 1) * 128], start=False, stop=True).then_inc(s_mh, 1)
            ctx.t.matmul(su_bank[:, 0:128], b["kp"][:, h * 128:(h + 1) * 128], vs,
                         start=True, stop=True).then_inc(s_mh, 1)
            ctx.v.wait(s_mh, mh0 + 3 * h + 3)
            ctx.v.scalar_tensor_tensor(b["S"][:, h * 128:(h + 1) * 128],
                                       b["S"][:, h * 128:(h + 1) * 128],
                                       b["lam128"][:, h:h + 1], su_bank[:, 0:128],
                                       op0=ALU.mult, op1=ALU.add).then_inc(s_sd, 1)
            ctx.s.wait(s_sd, sd0 + 3 * h + 3)
            ctx.s.wait(s_mh, mh0 + 3 * h + 2)
            ctx.s.activation(b["S_bf"][:, h * 128:(h + 1) * 128],
                             b["S"][:, h * 128:(h + 1) * 128], AF.Copy,
                             bias=0.0, scale=1.0).then_inc(s_te, 1)
        ctx.s.wait(s_mh, s_mh.val)
        ctx.s.wait(ctx.sem("st_o"), ctx.sem("st_o").val)
        for bb in range(4):
            ctx.s.activation(b["o_sb"][:, bb * 512:(bb + 1) * 512], ps[2 + bb][:, 0:512],
                             AF.Copy, bias=0.0, scale=1.0).then_inc(s_oe, 1)
        o_out_fn(pi * (TOKW // 128) + cc)




def emit_attn_local_phase(ctx, b, hw, cfg):
    s_o = ctx.sem("st_o")
    s_oe = ctx.sem("ac_a")
    def o_out_fn(c):
        ctx.g.wait(s_oe, s_oe.val)
        ctx.g.wait(s_o, s_o.val)
        ctx.g.dma_start(out=hw["o_out"][c * 128:(c + 1) * 128, :],
                        in_=b["o_sb"][:, :]).then_inc(s_o, 16)

    for pi in range(NPASS):
        ctx.run_bg(20)
        emit_attn_proj(ctx, b, hw, cfg, pi)
        ctx.g.wait(ctx.sem("ac_a"), ctx.sem("ac_a").val)
        ctx.g.wait(s_o, s_o.val)
        ctx.g.dma_start(out=_cm(hw["qT_out"][:, pi * TOKW:(pi + 1) * TOKW]),
                        in_=_sb3(b["qT"][:, :], H)).then_inc(s_o, 16)
        if cfg.get("gate_full"):
            for tt in range(TOKW // 128):
                ctx.g.dma_start(out=hw["gate_out"][pi * TOKW + tt * 128: pi * TOKW + (tt + 1) * 128, :],
                                in_=b["gate_tm"][:, tt * D:(tt + 1) * D]).then_inc(s_o, 16)
        emit_attn_chunks(ctx, b, cfg, pi, o_out_fn)
        ctx.s.wait(s_o, s_o.val)
        ctx.t.wait(s_o, s_o.val)
    ctx.g.wait(ctx.sem("ve_c"), ctx.sem("ve_c").val)
    ctx.g.wait(s_o, s_o.val)
    ctx.g.dma_start(out=hw["T_out"][:, :], in_=b["S"][:, :]).then_inc(s_o, 16)
    ctx.g.wait(s_o, s_o.val)


def build_attn_local(cfg):
    nc = bass.Bass("TRN2", target_bir_lowering=False, debug=False)
    hw = {}
    hw["xT_in"] = nc.dram_tensor("xT_in", [D, T], F32, kind="ExternalInput").ap()
    for wn in ("Wq", "Wk", "Wv"):
        hw[wn] = nc.dram_tensor(wn, [D, D], F32, kind="ExternalInput").ap()
    if cfg.get("gate_full"):
        hw["Wg1a"] = nc.dram_tensor("Wg1a", [D, D], F32, kind="ExternalInput").ap()
    hw["ldsrc"] = nc.dram_tensor("ldsrc", [1, H], F32, kind="ExternalInput").ap()
    hw["identity"] = nc.dram_tensor("identity", [128, 128], F32, kind="ExternalInput").ap()
    if cfg.get("biases"):
        hw["bq"] = nc.dram_tensor("bq", [128, H], F32, kind="ExternalInput").ap()
        hw["bk"] = nc.dram_tensor("bk", [128, H], F32, kind="ExternalInput").ap()
        hw["bv_row"] = nc.dram_tensor("bv_row", [1, D], F32, kind="ExternalInput").ap()
        hw["bg_row"] = nc.dram_tensor("bg_row", [1, D], F32, kind="ExternalInput").ap()
    hw["o_out"] = nc.dram_tensor("o_out", [T, D], F32, kind="ExternalOutput").ap()
    hw["qT_out"] = nc.dram_tensor("qT_out", [D, T], BF16, kind="ExternalOutput").ap()
    hw["T_out"] = nc.dram_tensor("T_out", [128, H * 128], F32, kind="ExternalOutput").ap()
    if cfg.get("gate_full"):
        hw["gate_out"] = nc.dram_tensor("gate_out", [T, D], BF16, kind="ExternalOutput").ap()

    from contextlib import ExitStack
    with ExitStack() as stack:
        arh = stack.enter_context(nc.sbuf_tensor("arena", [128, 48000], I32))
        ps = [stack.enter_context(nc.psum_tensor("ps%d" % i, [128, 512], F32)) for i in range(8)]
        ps = [p.ap() for p in ps]
        ar = Arena(arh, 48000)
        base = alloc_base(ar, ps)
        b = alloc_attn_bufs(ar, ps, base)

        def emit(ctx):
            emit_base_init(ctx, b, hw)
            emit_attn_tables(ctx, b, hw, cfg)
            emit_attn_local_phase(ctx, b, hw, cfg)

        build_program(nc, emit)
    return nc


def alloc_finish_bufs(ar, ps, base):
    b = alloc_attn_bufs(ar, ps, base, recur=False)
    b["tj"] = b["wsl"]
    b["kvb"] = ar.carve([128, 8], F32)
    b["kvs"] = ar.carve([128, 8], F32)
    b["Ej"] = ar.carve([128, H], F32)
    b["ldc"] = ar.carve([128, H], F32)
    b["qd2"] = ar.carve([128, 128], BF16)
    b["o_ld"] = ar.carve([128, D], F32)
    b["qTl"] = ar.carve([128, H * 128], BF16)
    b["sig"] = ar.carve([128, D], BF16)
    b["g1T"] = ar.carve([128, TOKW], BF16)
    b["g1w"] = ar.carve([128, DCH * 128], BF16)
    b["wg2b"] = ar.carve([128, D], BF16)
    b["ogT"] = ar.carve([128, H * TOKW], BF16)
    b["rst"] = ar.carve([128, H], F32)
    b["osq"] = ar.carve([128, D], F32)
    b["nw0_sb"] = ar.carve([128, DCH], F32)
    b["bo_sb"] = ar.carve([128, DCH], F32)
    b["st2"] = ar.carve([128, 4 * H], F32)
    return b


def emit_attn_finish(ctx, b, hw, cfg):
    s_d = ctx.sem("dma_b")
    s_ol = ctx.sem("dma_c")
    s_v = ctx.sem("ve_a")
    s_a = ctx.sem("ac_b")
    s_mm = ctx.sem("mm_a")
    s_ev = ctx.sem("ac_a")
    s_tp = ctx.sem("mm_c")
    s_sd = ctx.sem("ve_c")
    s_o = ctx.sem("st_o")
    s_wd = ctx.sem("dma_a")
    grms = cfg.get("norm") == "grms"
    ps = b["ps"]
    ctx.q.dma_start(out=b["kvb"][:, :], in_=hw["kvec"][0:1, :].to_broadcast([128, 8])).then_inc(s_d, 16)
    ctx.v.wait(s_d, s_d.val)
    ctx.v.tensor_scalar(b["kvs"][:, :], b["kvb"][:, :], 1024.0, None, op0=ALU.mult).then_inc(s_v, 1)
    ctx.q.wait(s_d, s_d.val)
    ctx.q.wait(ctx.sem("cc_b"), ctx.sem("cc_b").val)
    for j in range(NCORES):
        ctx.q.wait(s_sd, s_sd.val)
        ctx.q.dma_start(out=b["tj"][:, 0:D], in_=hw["Tall"][j * 128:(j + 1) * 128, :]).then_inc(s_d, 16)
        ctx.s.wait(s_d, s_d.val)
        ctx.s.wait(s_v, s_v.val)
        ctx.s.wait(s_sd, s_sd.val)
        ctx.s.activation(b["Ej"][:, :], b["ld_b"][:, :], AF.Exp, bias=0.0,
                         scale=b["kvs"][:, j:j + 1]).then_inc(s_a, 1)
        ctx.v.wait(s_a, s_a.val)
        for h in range(H):
            ctx.v.scalar_tensor_tensor(b["S"][:, h * 128:(h + 1) * 128],
                                       b["tj"][:, h * 128:(h + 1) * 128],
                                       b["Ej"][:, h:h + 1],
                                       b["S"][:, h * 128:(h + 1) * 128],
                                       op0=ALU.mult, op1=ALU.add).then_inc(s_sd, 1)
    ctx.s.wait(s_sd, s_sd.val)
    ctx.s.activation(b["S_bf"][:, :], b["S"][:, :], AF.Copy, bias=0.0, scale=1.0).then_inc(s_a, 1)
    ctx.q.wait(s_d, s_d.val)
    if grms:
        ctx.q.dma_start(out=b["nw0_sb"][:, :], in_=hw["nw0"][:, :]).then_inc(s_d, 16)
        ctx.q.wait(s_v, s_v.val)
        ctx.q.dma_start(out=_sb3(b["wsl"][:, 0:D], DCH), in_=_cm(hw["Wg1"][:, :])).then_inc(s_d, 16)
        ctx.v.wait(s_d, s_d.val)
        ctx.v.tensor_copy(b["g1w"][:, :], b["wsl"][:, 0:D]).then_inc(s_v, 1)
        ctx.q.wait(s_v, s_v.val)
        ctx.q.dma_start(out=b["wsl"][:, 0:D], in_=hw["Wg2"][:, :]).then_inc(s_d, 16)
        ctx.v.wait(s_d, s_d.val)
        ctx.v.tensor_copy(b["wg2b"][:, :], b["wsl"][:, 0:D]).then_inc(s_v, 1)
    else:
        ctx.q.dma_start(out=b["bo_sb"][:, :], in_=hw["bo"][:, :]).then_inc(s_d, 16)
        ctx.v.memset(b["st2"][:, :], 0.0).then_inc(s_v, 1)
    ctx.t.wait(s_v, s_v.val)
    ctx.t.wait(s_a, s_a.val)

    xch = [b["x"][:, c * T:(c + 1) * T] for c in range(DCH)]
    for pi in range(NPASS):
        ctx.run_bg(20)
        if grms:
            emit_ln(ctx, b, xch, b["xn_b"], TOKW, pi * TOKW, rms_fold=False, w_col=None,
                    dst_stride=TOKW, dst_off=0, a_s=b["a_s"], b_s=b["b_s"])
            ctx.t.wait(ctx.sem("ve_b"), ctx.sem("ve_b").val)
            ctx.t.wait(s_ev, s_ev.val)
            pb = ps[6][:, 0:TOKW]
            for c in range(DCH):
                mm = ctx.t.matmul(pb, b["g1w"][:, c * 128:(c + 1) * 128],
                                  b["xn_b"][:, c * TOKW:(c + 1) * TOKW],
                                  start=(c == 0), stop=(c == DCH - 1))
                if c == DCH - 1:
                    mm.then_inc(s_mm, 1)
            ctx.s.wait(s_mm, s_mm.val)
            ctx.s.activation(b["g1T"][:, :], pb, AF.Copy, bias=0.0, scale=1.0).then_inc(s_ev, 1)
        for cc in range(TOKW // 128):
            c = pi * (TOKW // 128) + cc
            ctx.q.wait(s_sd, s_sd.val)
            ctx.q.wait(ctx.sem("ac_a"), ctx.sem("ac_a").val)
            ctx.q.dma_start(out=b["o_ld"][:, :], in_=hw["o_in"][c * 128:(c + 1) * 128, :]).then_inc(s_ol, 16)
            ctx.q.dma_start(out=_sb3(b["qTl"][:, :], H),
                            in_=_cm(hw["qT_in"][:, c * 128:(c + 1) * 128])).then_inc(s_ol, 16)
            ctx.v.wait(s_a, s_a.val)
            ctx.v.tensor_scalar(b["ldc"][:, :], b["ld_b"][:, :], 128.0 * c, None, op0=ALU.mult).then_inc(s_sd, 1)
            ctx.s.wait(s_sd, s_sd.val)
            oe0 = s_ev.val
            for h in range(H):
                ctx.s.wait(s_sd, s_sd.val)
                ctx.s.activation(b["ln_tmp"][:, 0:128], b["iota_r"][:, :], AF.Exp,
                                 bias=b["ldc"][:, h:h + 1],
                                 scale=b["ld_b"][:, h:h + 1]).then_inc(s_a, 1)
                ctx.v.wait(s_a, s_a.val)
                ctx.v.wait(s_ol, s_ol.val)
                ctx.v.wait(s_mm, s_mm.val)
                ctx.v.tensor_tensor(b["qd2"][:, :], b["qTl"][:, h * 128:(h + 1) * 128],
                                    b["ln_tmp"][:, 0:128], op=ALU.mult).then_inc(s_sd, 1)
                ctx.t.wait(s_sd, s_sd.val)
                if h % 4 == 0:
                    ctx.t.wait(s_ev, oe0)
                ctx.t.matmul(ps[2 + h // 4][:, (h % 4) * 128:(h % 4 + 1) * 128],
                             b["qd2"][:, :], b["S_bf"][:, h * 128:(h + 1) * 128],
                             start=True, stop=True).then_inc(s_mm, 1)
            if grms:
                for eg in range(4):
                    pbg = ps[0][:, 0:512]
                    ctx.t.wait(s_ev, s_ev.val)
                    mm = ctx.t.matmul(pbg, b["g1T"][:, cc * 128:(cc + 1) * 128],
                                      b["wg2b"][:, eg * 512:(eg + 1) * 512],
                                      start=True, stop=True)
                    mm.then_inc(s_mm, 1)
                    ctx.s.wait(s_mm, s_mm.val)
                    ctx.s.activation(b["sig"][:, eg * 512:(eg + 1) * 512], pbg, AF.Sigmoid,
                                     bias=0.0, scale=1.0).then_inc(s_ev, 1)
            ctx.v.wait(s_mm, s_mm.val)
            ctx.v.wait(s_ev, s_ev.val)
            for bb in range(4):
                ctx.v.tensor_tensor(b["o_sb"][:, bb * 512:(bb + 1) * 512],
                                    b["o_ld"][:, bb * 512:(bb + 1) * 512],
                                    ps[2 + bb][:, 0:512], op=ALU.add).then_inc(s_sd, 1)
            if grms:
                for bb in range(4):
                    ctx.v.tensor_tensor(b["o_sb"][:, bb * 512:(bb + 1) * 512],
                                        b["o_sb"][:, bb * 512:(bb + 1) * 512],
                                        b["sig"][:, bb * 512:(bb + 1) * 512], op=ALU.mult).then_inc(s_sd, 1)
                ctx.s.wait(s_sd, s_sd.val)
                ctx.s.activation(b["osq"][:, :], b["o_sb"][:, :], AF.Square, bias=0.0, scale=1.0).then_inc(s_a, 1)
                ctx.v.wait(s_a, s_a.val)
                ctx.v.reduce_sum(b["rst"][:, :], b["osq"][:, :].rearrange("p (h e) -> p h e", h=H),
                                 axis=AX.X).then_inc(s_sd, 1)
                ctx.s.wait(s_sd, s_sd.val)
                ctx.s.activation(b["rst"][:, :], b["rst"][:, :], AF.Ln,
                                 bias=b["cst"]["eps_ln"], scale=1.0 / 128.0).then_inc(s_a, 1)
                ctx.s.activation(b["rst"][:, :], b["rst"][:, :], AF.Exp, bias=0.0,
                                 scale=b["cst"]["mhalf"]).then_inc(s_a, 1)
                ctx.v.wait(s_a, s_a.val)
                for h in range(H):
                    ctx.v.tensor_scalar(b["sig"][:, h * 128:(h + 1) * 128],
                                        b["o_sb"][:, h * 128:(h + 1) * 128],
                                        b["rst"][:, h:h + 1], None, op0=ALU.mult).then_inc(s_sd, 1)
                src_tm = b["sig"]
            else:
                ctx.g.wait(s_sd, s_sd.val)
                ctx.g.wait(s_o, s_o.val)
                ctx.g.dma_start(out=hw["ret_out"][c * 128:(c + 1) * 128, :],
                                in_=b["o_sb"][:, :]).then_inc(s_o, 16)
                ctx.s.wait(s_sd, s_sd.val)
                ctx.s.activation(b["osq"][:, :], b["o_sb"][:, :], AF.Square, bias=0.0, scale=1.0).then_inc(s_a, 1)
                ctx.v.reduce_sum(b["st2"][:, 0:H], b["o_sb"][:, :].rearrange("p (h e) -> p h e", h=H),
                                 axis=AX.X).then_inc(s_sd, 1)
                ctx.v.wait(s_a, s_a.val)
                ctx.v.reduce_sum(b["st2"][:, H:2 * H], b["osq"][:, :].rearrange("p (h e) -> p h e", h=H),
                                 axis=AX.X).then_inc(s_sd, 1)
                ctx.v.tensor_tensor(b["st2"][:, 2 * H:3 * H], b["st2"][:, 2 * H:3 * H],
                                    b["st2"][:, 0:H], op=ALU.add)
                ctx.v.tensor_tensor(b["st2"][:, 3 * H:4 * H], b["st2"][:, 3 * H:4 * H],
                                    b["st2"][:, H:2 * H], op=ALU.add).then_inc(s_sd, 1)
                continue
            tp0 = s_tp.val
            te0 = s_a.val
            ctx.t.wait(s_sd, s_sd.val)
            ctx.t.wait(s_a, te0)
            for h in range(H):
                pbv = ps[6 + (h % 2)].bitcast(BF16)[:, 0:128]
                if h >= 2:
                    ctx.t.wait(s_a, te0 + h - 1)
                ctx.t.transpose(pbv, src_tm[:, h * 128:(h + 1) * 128], idb := b["idb"][:, :]).then_inc(s_tp, 1)
                ctx.s.wait(s_tp, tp0 + h + 1)
                ctx.s.activation(b["ogT"][:, h * TOKW + cc * 128: h * TOKW + (cc + 1) * 128],
                                 pbv, AF.Copy, bias=0.0, scale=1.0).then_inc(s_a, 1)
        if not grms:
            continue
        emit_wo_proj(ctx, b, hw, cfg, pi)
    if not grms:
        ctx.g.wait(s_sd, s_sd.val)
        ctx.g.wait(s_o, s_o.val)
        ctx.g.dma_start(out=hw["st_out"][:, :], in_=b["st2"][:, :]).then_inc(s_o, 16)
        ctx.g.wait(s_o, s_o.val)


def emit_wo_proj(ctx, b, hw, cfg, pi, src_key="ogT"):
    s_wd = ctx.sem("dma_a")
    s_cv = ctx.sem("ve_a")
    s_mm = ctx.sem("mm_a")
    s_a = ctx.sem("ac_b")
    s_res = ctx.sem("ve_b")
    grms = cfg.get("norm") == "grms"
    for dc in range(DCH):
        ctx.q.wait(s_cv, s_cv.val)
        ctx.q.dma_start(out=_sb3(b["wsl"][:, 0:D], DCH),
                        in_=_cm(hw["Wo"][:, dc * 128:(dc + 1) * 128])).then_inc(s_wd, 16)
        ctx.v.wait(s_wd, s_wd.val)
        ctx.v.wait(s_mm, s_mm.val)
        if grms:
            for c in range(DCH):
                ctx.v.tensor_scalar(b["wslb"][:, c * 128:(c + 1) * 128],
                                    b["wsl"][:, c * 128:(c + 1) * 128],
                                    b["nw0_sb"][:, c:c + 1], None, op0=ALU.mult).then_inc(s_cv, 1)
        else:
            ctx.v.tensor_copy(b["wslb"][:, 0:D], b["wsl"][:, 0:D]).then_inc(s_cv, 1)
        ctx.t.wait(s_cv, s_cv.val)
        ctx.t.wait(s_res, s_res.val)
        ctx.t.wait(ctx.sem("ac_a"), ctx.sem("ac_a").val)
        pb = b["ps"][0 if dc % 2 == 0 else 1][:, 0:TOKW]
        for c in range(DCH):
            mm = ctx.t.matmul(pb, b["wslb"][:, c * 128:(c + 1) * 128],
                              b[src_key][:, c * TOKW:(c + 1) * TOKW],
                              start=(c == 0), stop=(c == DCH - 1))
            if c == DCH - 1:
                mm.then_inc(s_mm, 1)
        ctx.v.wait(s_mm, s_mm.val)
        xsl = b["x"][:, dc * T + pi * TOKW: dc * T + (pi + 1) * TOKW]
        if grms:
            ctx.v.tensor_tensor(xsl, xsl, pb, op=ALU.add).then_inc(s_res, 1)
        else:
            ctx.v.scalar_tensor_tensor(xsl, pb, b["bo_sb"][:, dc:dc + 1],
                                       xsl, op0=ALU.add, op1=ALU.add).then_inc(s_res, 1)


def build_attn_finish(cfg):
    nc = bass.Bass("TRN2", target_bir_lowering=False, debug=False)
    hw = {}
    hw["xT_in"] = nc.dram_tensor("xT_in", [D, T], F32, kind="ExternalInput").ap()
    hw["o_in"] = nc.dram_tensor("o_in", [T, D], F32, kind="ExternalInput").ap()
    hw["qT_in"] = nc.dram_tensor("qT_in", [D, T], BF16, kind="ExternalInput").ap()
    hw["Tall"] = nc.dram_tensor("Tall", [NCORES * 128, H * 128], F32, kind="ExternalInput").ap()
    hw["kvec"] = nc.dram_tensor("kvec", [1, NCORES], F32, kind="ExternalInput").ap()
    hw["ldsrc"] = nc.dram_tensor("ldsrc", [1, H], F32, kind="ExternalInput").ap()
    hw["identity"] = nc.dram_tensor("identity", [128, 128], F32, kind="ExternalInput").ap()
    grms = cfg.get("norm") == "grms"
    if grms:
        hw["Wg1"] = nc.dram_tensor("Wg1", [D, 128], F32, kind="ExternalInput").ap()
        hw["Wg2"] = nc.dram_tensor("Wg2", [128, D], F32, kind="ExternalInput").ap()
        hw["nw0"] = nc.dram_tensor("nw0", [128, DCH], F32, kind="ExternalInput").ap()
        hw["Wo"] = nc.dram_tensor("Wo", [D, D], F32, kind="ExternalInput").ap()
        hw["xT_out"] = nc.dram_tensor("xT_out", [D, T], F32, kind="ExternalOutput").ap()
    else:
        hw["ret_out"] = nc.dram_tensor("ret_out", [T, D], F32, kind="ExternalOutput").ap()
        hw["st_out"] = nc.dram_tensor("st_out", [128, 4 * H], F32, kind="ExternalOutput").ap()
        hw["bo"] = nc.dram_tensor("bo", [128, DCH], F32, kind="ExternalInput").ap()

    from contextlib import ExitStack
    with ExitStack() as stack:
        arh = stack.enter_context(nc.sbuf_tensor("arena", [128, 48000], I32))
        ps = [stack.enter_context(nc.psum_tensor("ps%d" % i, [128, 512], F32)) for i in range(8)]
        ps = [p.ap() for p in ps]
        ar = Arena(arh, 48000)
        base = alloc_base(ar, ps)
        b = alloc_finish_bufs(ar, ps, base)

        def emit(ctx):
            s_d = ctx.sem("dma_b")
            s_o = ctx.sem("st_o")
            emit_base_init(ctx, b, hw)
            emit_attn_tables(ctx, b, hw, dict(cfg, biases=False, finish=True))
            emit_attn_finish(ctx, b, hw, cfg)
            if grms:
                ctx.g.wait(ctx.sem("ve_b"), ctx.sem("ve_b").val)
                ctx.g.wait(s_o, s_o.val)
                for qq in range(4):
                    ctx.g.dma_start(out=_cm(hw["xT_out"][qq * 512:(qq + 1) * 512, :]),
                                    in_=_sb3(b["x"][:, qq * 4 * T:(qq + 1) * 4 * T], 4)).then_inc(s_o, 16)
                ctx.g.wait(s_o, s_o.val)

        build_program(nc, emit)
    return nc


def alloc_l1norm_bufs(ar, ps, base):
    b = dict(base)
    b["stc"] = ar.carve([128, 8 * 4 * H], F32)
    b["sac"] = ar.carve([128, 4 * H], F32)
    b["strow"] = ar.carve([1, 4 * H], F32)
    b["mcol"] = ar.carve([128, H], F32)
    b["rcol"] = ar.carve([128, H], F32)
    b["gnw_b"] = ar.carve([128, D], F32)
    b["gnb_b"] = ar.carve([128, D], F32)
    b["gnrow"] = ar.carve([1, 2 * D], F32)
    b["ret"] = ar.carve([128, D], F32)
    b["gld"] = ar.carve([128, D], BF16)
    b["og"] = ar.carve([128, D], BF16)
    b["ogT"] = ar.carve([128, H * TOKW], BF16)
    b["wsl"] = ar.carve([128, DCH * 128], F32)
    b["wslb"] = ar.carve([128, DCH * 128], BF16)
    b["bo_sb"] = ar.carve([128, DCH], F32)
    b["ln_tmp"] = ar.carve([128, 2 * TOKW], F32)
    return b


def emit_l1_norm(ctx, b, hw):
    ps = b["ps"]
    s_d = ctx.sem("dma_b")
    s_ol = ctx.sem("dma_c")
    s_v = ctx.sem("ve_a")
    s_a = ctx.sem("ac_b")
    s_mm = ctx.sem("mm_a")
    s_tp = ctx.sem("mm_c")
    s_sd = ctx.sem("ve_c")
    s_o = ctx.sem("st_o")
    ctx.q.wait(ctx.sem("ve_b"), ctx.sem("ve_b").val)
    ctx.q.dma_start(out=_sb3(b["stc"][:, :], 8), in_=_cm(hw["stats"][:, :])).then_inc(s_d, 16)
    ctx.q.dma_start(out=b["bo_sb"][:, :], in_=hw["bo"][:, :]).then_inc(s_d, 16)
    ctx.q.dma_start(out=b["gnrow"][0:1, 0:D], in_=hw["gnw"][:, :]).then_inc(s_d, 16)
    ctx.q.dma_start(out=b["gnrow"][0:1, D:2 * D], in_=hw["gnb"][:, :]).then_inc(s_d, 16)
    ctx.v.wait(s_d, s_d.val)
    ctx.v.tensor_copy(b["sac"][:, :], b["stc"][:, 0:4 * H])
    for j in range(1, 8):
        ctx.v.tensor_tensor(b["sac"][:, :], b["sac"][:, :],
                            b["stc"][:, j * 4 * H:(j + 1) * 4 * H], op=ALU.add).then_inc(s_v, 1)
    ctx.t.wait(s_v, s_v.val)
    ctx.t.matmul(ps[0][0:1, 0:4 * H], b["ones_col_f32"][:, :], b["sac"][:, :],
                 start=True, stop=True).then_inc(s_mm, 1)
    ctx.v.wait(s_mm, s_mm.val)
    Nn = float(SEQ * 128)
    st = b["strow"]
    mrow = st[0:1, 2 * H:3 * H]
    qrow = st[0:1, 3 * H:4 * H]
    ctx.v.tensor_scalar(mrow, ps[0][0:1, 2 * H:3 * H], 1.0 / Nn, None, op0=ALU.mult)
    ctx.v.tensor_scalar(qrow, ps[0][0:1, 3 * H:4 * H], 1.0 / Nn, None, op0=ALU.mult)
    ctx.v.tensor_tensor(st[0:1, 0:H], mrow, mrow, op=ALU.mult)
    ctx.v.tensor_tensor(st[0:1, 0:H], qrow, st[0:1, 0:H], op=ALU.subtract).then_inc(s_v, 1)
    ctx.s.wait(s_v, s_v.val)
    ctx.s.activation(st[0:1, H:2 * H], st[0:1, 0:H], AF.Ln,
                     bias=b["cst"]["eps_ln"][0:1, :], scale=1.0).then_inc(s_a, 1)
    ctx.s.wait(s_a, s_a.val)
    ctx.s.activation(st[0:1, 0:H], st[0:1, H:2 * H], AF.Exp, bias=0.0,
                     scale=b["cst"]["mhalf"][0:1, :]).then_inc(s_a, 1)
    ctx.t.wait(s_a, s_a.val)
    ctx.t.matmul(ps[1][:, 0:H], b["ones_row_f32"][:, :], mrow, start=True, stop=True)
    ctx.t.matmul(ps[2][:, 0:H], b["ones_row_f32"][:, :], st[0:1, 0:H], start=True, stop=True)
    for eg in range(4):
        ctx.t.wait(s_a, s_a.val)
        ctx.t.matmul(ps[3][:, 0:512],
                     b["ones_row_f32"][:, :], b["gnrow"][0:1, eg * 512:(eg + 1) * 512],
                     start=True, stop=True).then_inc(s_mm, 1)
        ctx.s.wait(s_mm, s_mm.val)
        ctx.s.activation(b["gnw_b"][:, eg * 512:(eg + 1) * 512], ps[3][:, 0:512],
                         AF.Copy, bias=0.0, scale=1.0).then_inc(s_a, 1)
        ctx.t.wait(s_a, s_a.val)
        ctx.t.matmul(ps[3][:, 0:512],
                     b["ones_row_f32"][:, :], b["gnrow"][0:1, D + eg * 512:D + (eg + 1) * 512],
                     start=True, stop=True).then_inc(s_mm, 1)
        ctx.s.wait(s_mm, s_mm.val)
        ctx.s.activation(b["gnb_b"][:, eg * 512:(eg + 1) * 512], ps[3][:, 0:512],
                         AF.Copy, bias=0.0, scale=1.0).then_inc(s_a, 1)
    ctx.t.matmul(ps[1][:, 0:H], b["ones_row_f32"][:, :], mrow, start=True, stop=True)
    ctx.t.matmul(ps[2][:, 0:H], b["ones_row_f32"][:, :], st[0:1, 0:H],
                 start=True, stop=True).then_inc(s_mm, 1)
    ctx.s.wait(s_mm, s_mm.val)
    ctx.s.activation(b["mcol"][:, :], ps[1][:, 0:H], AF.Copy, bias=0.0, scale=1.0)
    ctx.s.activation(b["rcol"][:, :], ps[2][:, 0:H], AF.Copy, bias=0.0, scale=1.0).then_inc(s_a, 1)
    ctx.v.wait(s_a, s_a.val)
    for pi in range(NPASS):
        for cc in range(TOKW // 128):
            c = pi * (TOKW // 128) + cc
            ctx.q.wait(s_sd, s_sd.val)
            ctx.q.wait(ctx.sem("ac_a"), ctx.sem("ac_a").val)
            ctx.q.dma_start(out=b["ret"][:, :], in_=hw["ret_in"][c * 128:(c + 1) * 128, :]).then_inc(s_ol, 16)
            ctx.q.dma_start(out=b["gld"][:, :], in_=hw["gate_in"][c * 128:(c + 1) * 128, :]).then_inc(s_ol, 16)
            ctx.v.wait(s_ol, s_ol.val)
            for h in range(H):
                ctx.v.tensor_scalar(b["ret"][:, h * 128:(h + 1) * 128],
                                    b["ret"][:, h * 128:(h + 1) * 128],
                                    b["mcol"][:, h:h + 1], b["rcol"][:, h:h + 1],
                                    op0=ALU.subtract, op1=ALU.mult)
            ctx.v.tensor_tensor(b["ret"][:, :], b["ret"][:, :], b["gnw_b"][:, :], op=ALU.mult)
            ctx.v.tensor_tensor(b["ret"][:, :], b["ret"][:, :], b["gnb_b"][:, :], op=ALU.add)
            ctx.v.tensor_tensor(b["og"][:, :], b["ret"][:, :], b["gld"][:, :], op=ALU.mult).then_inc(s_sd, 1)
            tp0 = s_tp.val
            a0 = ctx.sem("ac_a").val
            ctx.t.wait(s_sd, s_sd.val)
            ctx.t.wait(ctx.sem("ac_a"), a0)
            for h in range(H):
                pbv = ps[6 + (h % 2)].bitcast(BF16)[:, 0:128]
                if h >= 2:
                    ctx.t.wait(ctx.sem("ac_a"), a0 + h - 1)
                ctx.t.transpose(pbv, b["og"][:, h * 128:(h + 1) * 128],
                                b["idb"][:, :]).then_inc(s_tp, 1)
                ctx.s.wait(s_tp, tp0 + h + 1)
                ctx.s.activation(b["ogT"][:, h * TOKW + cc * 128: h * TOKW + (cc + 1) * 128],
                                 pbv, AF.Copy, bias=0.0, scale=1.0).then_inc(ctx.sem("ac_a"), 1)
        emit_wo_proj(ctx, b, hw, dict(norm="gn"), pi)


def build_l1_norm(cfg):
    nc = bass.Bass("TRN2", target_bir_lowering=False, debug=False)
    hw = {}
    hw["xT_in"] = nc.dram_tensor("xT_in", [D, T], F32, kind="ExternalInput").ap()
    hw["ret_in"] = nc.dram_tensor("ret_in", [T, D], F32, kind="ExternalInput").ap()
    hw["gate_in"] = nc.dram_tensor("gate_in", [T, D], BF16, kind="ExternalInput").ap()
    hw["stats"] = nc.dram_tensor("stats", [NCORES * 128, 4 * H], F32, kind="ExternalInput").ap()
    hw["gnw"] = nc.dram_tensor("gnw", [1, D], F32, kind="ExternalInput").ap()
    hw["gnb"] = nc.dram_tensor("gnb", [1, D], F32, kind="ExternalInput").ap()
    hw["Wo"] = nc.dram_tensor("Wo", [D, D], F32, kind="ExternalInput").ap()
    hw["bo"] = nc.dram_tensor("bo", [128, DCH], F32, kind="ExternalInput").ap()
    hw["identity"] = nc.dram_tensor("identity", [128, 128], F32, kind="ExternalInput").ap()
    hw["xT_out"] = nc.dram_tensor("xT_out", [D, T], F32, kind="ExternalOutput").ap()
    from contextlib import ExitStack
    with ExitStack() as stack:
        arh = stack.enter_context(nc.sbuf_tensor("arena", [128, 48000], I32))
        ps = [stack.enter_context(nc.psum_tensor("ps%d" % i, [128, 512], F32)) for i in range(8)]
        ps = [p.ap() for p in ps]
        ar = Arena(arh, 48000)
        base = alloc_base(ar, ps)
        b = alloc_l1norm_bufs(ar, ps, base)

        def emit(ctx):
            s_o = ctx.sem("st_o")
            emit_base_init(ctx, b, hw)
            emit_l1_norm(ctx, b, hw)
            ctx.g.wait(ctx.sem("ve_b"), ctx.sem("ve_b").val)
            ctx.g.wait(s_o, s_o.val)
            for qq in range(4):
                ctx.g.dma_start(out=_cm(hw["xT_out"][qq * 512:(qq + 1) * 512, :]),
                                in_=_sb3(b["x"][:, qq * 4 * T:(qq + 1) * 4 * T], 4)).then_inc(s_o, 16)
            ctx.g.wait(s_o, s_o.val)

        build_program(nc, emit)
    return nc


def _deq_plane(ctx, eng, pk_v, sc_v, dst, bufs, hi, W, chain_sem):
    r_i, t8, fa, fb, ft, fv = bufs
    if hi:
        eng.tensor_scalar(r_i[:, 0:W], pk_v, 4, 7, op0=ALU.logical_shift_right, op1=ALU.bitwise_and)
        eng.tensor_scalar(t8[:, 0:W], pk_v, 7, 1, op0=ALU.logical_shift_right, op1=ALU.bitwise_and)
        eng.tensor_scalar(fv[:, 0:W], t8[:, 0:W], -1.0, 0.5, op0=ALU.mult, op1=ALU.add)
    else:
        eng.tensor_scalar(r_i[:, 0:W], pk_v, 7, None, op0=ALU.bitwise_and)
        eng.tensor_scalar(t8[:, 0:W], pk_v, 8, None, op0=ALU.bitwise_and)
        eng.tensor_scalar(fv[:, 0:W], t8[:, 0:W], -0.125, 0.5, op0=ALU.mult, op1=ALU.add)
    eng.tensor_scalar(fa[:, 0:W], r_i[:, 0:W], 4.0, 0.0, op0=ALU.subtract, op1=ALU.max)
    eng.tensor_scalar(fb[:, 0:W], r_i[:, 0:W], 6.0, 0.0, op0=ALU.subtract, op1=ALU.max)
    eng.scalar_tensor_tensor(ft[:, 0:W], r_i[:, 0:W], 0.0, fa[:, 0:W], op0=ALU.add, op1=ALU.add)
    eng.scalar_tensor_tensor(ft[:, 0:W], fb[:, 0:W], 2.0, ft[:, 0:W], op0=ALU.mult, op1=ALU.add)
    eng.tensor_tensor(ft[:, 0:W], ft[:, 0:W], fv[:, 0:W], op=ALU.mult)
    eng.tensor_tensor(dst, ft[:, 0:W], sc_v, op=ALU.mult).then_inc(chain_sem, 1)


def _dequant_subtile(ctx, b, pk_hbm, sc_hbm, out_hbm, a, ns, hi_row_off, W):
    s_dv = ctx.sem("gp_a")
    i = b.setdefault("dq_i", 0)
    b["dq_i"] = i + 1
    par = i % 2
    s_ld = ctx.sem("dma_d" if par == 0 else "dma_e")
    s_st = ctx.sem("st_d" if par == 0 else "st_e")
    if i == 0:
        b["dq_base"] = (ctx.sem("dma_d").val, ctx.sem("dma_e").val, s_dv.val,
                        ctx.sem("st_d").val, ctx.sem("st_e").val)
    ld0e, ld0o, dv0, st0e, st0o = b["dq_base"]
    ld0 = ld0e if par == 0 else ld0o
    st0 = st0e if par == 0 else st0o
    k = i // 2
    pk = b["pk"]; sc = b["sc"]
    wlo = b["wlo"]; whi = b["whi"]
    po = par * W
    if i >= 2:
        ctx.q.wait(s_dv, dv0 + 2 * (i - 1))
    ctx.q.dma_start(out=pk[:, po:po + W],
                    in_=pk_hbm[a * 128:(a + 1) * 128, ns * W:(ns + 1) * W]).then_inc(s_ld, 16)
    for j in range(8):
        ctx.q.dma_start(out=sc[16 * j:16 * (j + 1), po:po + W],
                        in_=sc_hbm[8 * a + j:8 * a + j + 1, ns * W:(ns + 1) * W]
                        .to_broadcast([16, W])).then_inc(s_ld, 16)
    ctx.v.wait(s_ld, ld0 + 144 * (k + 1))
    if i >= 2:
        ctx.v.wait(s_st, st0 + 32 * k)
    pk_v = pk[:, po:po + W]
    sc_v = sc[:, po:po + W]
    _deq_plane(ctx, ctx.v, pk_v, sc_v, wlo[:, po:po + W], b["dqv"], 0, W, s_dv)
    _deq_plane(ctx, ctx.v, pk_v, sc_v, whi[:, po:po + W], b["dqg"], 1, W, s_dv)
    ctx.g.wait(s_dv, dv0 + 2 * (i + 1))
    ctx.g.dma_start(out=out_hbm[a * 128:(a + 1) * 128, ns * W:(ns + 1) * W],
                    in_=wlo[:, po:po + W]).then_inc(s_st, 16)
    ctx.g.dma_start(out=out_hbm[(hi_row_off + a) * 128:(hi_row_off + a + 1) * 128,
                                ns * W:(ns + 1) * W],
                    in_=whi[:, po:po + W]).then_inc(s_st, 16)


def emit_dequant_one(ctx, b, pk_hbm, sc_hbm, out_hbm, nchunks, width, hi_row_off):
    b.pop("dq_i", None)
    W = min(b["dq_w"], width)
    nsub = width // W
    for a in range(nchunks):
        for ns in range(nsub):
            _dequant_subtile(ctx, b, pk_hbm, sc_hbm, out_hbm, a, ns, hi_row_off, W)


def dequant_task_list(ctx, b, hw, cc):
    b.pop("dq_i", None)
    tasks = []
    rg = [list(range(NCORES))]
    def make_sub(pk_hbm, sc_hbm, out_hbm, a, ns, hioff, W):
        return lambda: _dequant_subtile(ctx, b, pk_hbm, sc_hbm, out_hbm, a, ns, hioff, W)
    def make_ag(inp, outp):
        def f():
            ctx.g.wait(ctx.sem("st_d"), ctx.sem("st_d").val)
            ctx.g.wait(ctx.sem("st_e"), ctx.sem("st_e").val)
            ctx.g.collective_compute("AllGather", ALU.bypass, replica_groups=rg,
                                     ins=[inp[:, :].opt()], outs=[outp[:, :].opt()]).then_inc(cc, 1)
        return f
    for (pkk, scc, outp, nch, width, hioff, agout) in (
            (hw["gWp"], hw["gSc"], hw["gq_p"], 8, 2048, 8, hw["WgT_ag"]),
            (hw["uWp"], hw["uSc"], hw["uq_p"], 8, 2048, 8, hw["WuT_ag"]),
            (hw["dWp"], hw["dSc"], hw["dq_p"], 64, 256, 64, hw["WdT_ag"])):
        W = min(b["dq_w"], width)
        for a in range(nch):
            for ns in range(width // W):
                tasks.append(make_sub(pkk, scc, outp, a, ns, hioff, W))
        tasks.append(make_ag(outp, agout))
    return tasks


def build_dequant():
    nc = bass.Bass("TRN2", target_bir_lowering=False, debug=False)
    hw = {}
    hw["gWp"] = nc.dram_tensor("gWp", [1024, 2048], I32, kind="ExternalInput").ap()
    hw["gSc"] = nc.dram_tensor("gSc", [64, 2048], F32, kind="ExternalInput").ap()
    hw["uWp"] = nc.dram_tensor("uWp", [1024, 2048], I32, kind="ExternalInput").ap()
    hw["uSc"] = nc.dram_tensor("uSc", [64, 2048], F32, kind="ExternalInput").ap()
    hw["dWp"] = nc.dram_tensor("dWp", [8192, 256], I32, kind="ExternalInput").ap()
    hw["dSc"] = nc.dram_tensor("dSc", [512, 256], F32, kind="ExternalInput").ap()
    hw["gq"] = nc.dram_tensor("gq", [2048, 2048], BF16, kind="ExternalOutput").ap()
    hw["uq"] = nc.dram_tensor("uq", [2048, 2048], BF16, kind="ExternalOutput").ap()
    hw["dq"] = nc.dram_tensor("dq", [16384, 256], BF16, kind="ExternalOutput").ap()
    from contextlib import ExitStack
    with ExitStack() as stack:
        arh = stack.enter_context(nc.sbuf_tensor("arena", [128, 26000], I32))
        ar = Arena(arh, 26000)
        b = alloc_dequant_bufs(ar, W=512)

        def emit(ctx):
            emit_dequant_one(ctx, b, hw["gWp"], hw["gSc"], hw["gq"], 8, 2048, 8)
            emit_dequant_one(ctx, b, hw["uWp"], hw["uSc"], hw["uq"], 8, 2048, 8)
            emit_dequant_one(ctx, b, hw["dWp"], hw["dSc"], hw["dq"], 64, 256, 64)
            s_st = ctx.sem("st_d")
            ctx.g.wait(s_st, s_st.val)

        build_program(nc, emit)
    return nc



def emit_l1ffn(ctx, b, hw, slab_fn=None):
    s_wd = ctx.sem("dma_a")
    s_mm = ctx.sem("mm_a")
    s_gl = ctx.sem("ve_c")
    s_ga = ctx.sem("ac_a")
    s_ae = ctx.sem("ac_b")
    s_res = ctx.sem("ve_b")
    ps = b["ps"]
    xch = [b["x"][:, c * T:(c + 1) * T] for c in range(DCH)]
    for half in range(2):
        emit_ln(ctx, b, xch, b["xn_b"], 512, half * 512, rms_fold=False, w_col=None,
                dst_stride=T, dst_off=half * 512, a_s=b["a_s"], b_s=b["b_s"])
    xn_ready = s_res.val
    gsl = b["gsl"]; usl = b["usl"]; dsl = b["dsl"]
    inter = b["inter"]
    ft = b["ft"]; f2 = b["f2"]
    gbuf = b["gbuf"]; ubuf = b["ubuf"]
    for seg in range(8):
        mm_seg0 = s_mm.val
        ae0 = s_ae.val
        gl0 = s_gl.val
        for nci in range(16):
            nn = seg * 16 + nci
            slot = (nci % 2) * DCH * 128
            ctx.q.wait(s_mm, s_mm.val if nci < 2 else s_mm.val - 2)
            g_src = slab_fn("g", nn) if slab_fn else _cm(hw["WgT"][:, nn * 128:(nn + 1) * 128])
            u_src = slab_fn("u", nn) if slab_fn else _cm(hw["WuT"][:, nn * 128:(nn + 1) * 128])
            ctx.q.dma_start(out=_sb3(gsl[:, slot:slot + DCH * 128], DCH), in_=g_src).then_inc(s_wd, 16)
            ctx.q.dma_start(out=_sb3(usl[:, slot:slot + DCH * 128], DCH), in_=u_src).then_inc(s_wd, 16)
            ctx.t.wait(s_wd, s_wd.val)
            ctx.t.wait(s_ap := ctx.sem("ve_b"), xn_ready)
            ctx.t.wait(s_ae, ae0 + 4 * nci)
            for half in range(2):
                for c in range(DCH):
                    ctx.t.matmul(ps[half][:, 0:512], gsl[:, slot + c * 128: slot + (c + 1) * 128],
                                 b["xn_b"][:, c * T + half * 512: c * T + (half + 1) * 512],
                                 start=(c == 0), stop=(c == DCH - 1))
                for c in range(DCH):
                    mm = ctx.t.matmul(ps[2 + half][:, 0:512], usl[:, slot + c * 128: slot + (c + 1) * 128],
                                      b["xn_b"][:, c * T + half * 512: c * T + (half + 1) * 512],
                                      start=(c == 0), stop=(c == DCH - 1))
                    if c == DCH - 1:
                        mm.then_inc(s_mm, 1)
            ctx.s.wait(s_mm, s_mm.val)
            if seg == 0 and nci == 0:
                ctx.s.wait(ctx.sem("ve_b"), xn_ready)
            if nci >= 2:
                ctx.s.wait(s_gl, gl0 + 4 * (nci - 1))
            for half in range(2):
                eslot = (nci % 2) * 1024 + half * 512
                ctx.s.activation(gbuf[:, eslot:eslot + 512], ps[half][:, 0:512],
                                 AF.Copy, bias=0.0, scale=1.0)
                ctx.s.activation(ubuf[:, eslot:eslot + 512], ps[2 + half][:, 0:512],
                                 AF.Copy, bias=0.0, scale=1.0).then_inc(s_ae, 2)
            for half in range(2):
                eslot = (nci % 2) * 1024 + half * 512
                gv = gbuf[:, eslot:eslot + 512]
                uv = ubuf[:, eslot:eslot + 512]
                dst = inter[:, nci * T + half * 512: nci * T + (half + 1) * 512]
                ctx.s.activation(ft[:, 0:512], gv, AF.Square, bias=0.0, scale=1.0).then_inc(s_ga, 1)
                ctx.v.wait(s_ga, s_ga.val)
                ctx.v.tensor_scalar(ft[:, 0:512], ft[:, 0:512], 0.044715, 1.0, op0=ALU.mult, op1=ALU.add)
                ctx.v.tensor_tensor(ft[:, 0:512], ft[:, 0:512], gv, op=ALU.mult).then_inc(s_gl, 1)
                ctx.s.wait(s_gl, s_gl.val)
                ctx.s.activation(f2[:, 0:512], ft[:, 0:512], AF.Sigmoid, bias=0.0,
                                 scale=b["cst"]["cgelu"]).then_inc(s_ga, 1)
                ctx.v.wait(s_ga, s_ga.val)
                ctx.v.tensor_tensor(f2[:, 0:512], f2[:, 0:512], gv, op=ALU.mult)
                ctx.v.tensor_tensor(dst, f2[:, 0:512], uv, op=ALU.mult).then_inc(s_gl, 1)
        res_base = s_res.val
        for h in range(DCH):
            dslot = (h % 2) * DCH * 128
            ctx.q.wait(s_mm, s_mm.val if h < 2 else s_mm.val - 2)
            d_src = (slab_fn("d", (seg, h)) if slab_fn else
                     _cm(hw["WdT"][seg * 2048:(seg + 1) * 2048, h * 128:(h + 1) * 128]))
            ctx.q.dma_start(out=_sb3(dsl[:, dslot:dslot + DCH * 128], DCH), in_=d_src).then_inc(s_wd, 16)
            ctx.t.wait(s_wd, s_wd.val)
            ctx.t.wait(s_gl, s_gl.val)
            for half in range(2):
                i = h * 2 + half
                pd = ps[4 + (i % 2)][:, 0:512]
                if i >= 2:
                    ctx.t.wait(s_res, res_base + i - 1)
                for nci in range(DCH):
                    mm = ctx.t.matmul(pd, dsl[:, dslot + nci * 128: dslot + (nci + 1) * 128],
                                      inter[:, nci * T + half * 512: nci * T + (half + 1) * 512],
                                      start=(nci == 0), stop=(nci == DCH - 1))
                    if nci == DCH - 1:
                        mm.then_inc(s_mm, 1)
                ctx.v.wait(s_mm, s_mm.val)
                xsl = xch[h][:, half * 512:(half + 1) * 512]
                ctx.v.scalar_tensor_tensor(xsl, pd, 0.5, xsl, op0=ALU.mult, op1=ALU.add).then_inc(s_res, 1)


def alloc_l1ffn_bufs(ar, ps, base):
    b = dict(base)
    b["xn_b"] = ar.carve([128, DCH * T], BF16)
    b["inter"] = ar.carve([128, DCH * T], BF16)
    b["gsl"] = ar.carve([128, 2 * DCH * 128], BF16)
    b["usl"] = ar.carve([128, 2 * DCH * 128], BF16)
    b["dsl"] = ar.carve([128, 2 * DCH * 128], BF16)
    b["ft"] = ar.carve([128, 512], F32)
    b["f2"] = ar.carve([128, 512], F32)
    b["gbuf"] = ar.carve([128, 2048], F32)
    b["ubuf"] = ar.carve([128, 2048], F32)
    b["a_s"] = b["ubuf"][:, 0:512]
    b["b_s"] = b["ubuf"][:, 512:1024]
    b["ln_xq"] = b["gbuf"][:, 0:1024]
    b["ln_st"] = b["inter"][0:1, :].bitcast(F32)[0:1, 0:4096]
    b["ln_tmp"] = ar.carve([128, 512], F32)
    return b


def build_l1ffn():
    nc = bass.Bass("TRN2", target_bir_lowering=False, debug=False)
    hw = {}
    hw["xT_in"] = nc.dram_tensor("xT_in", [D, T], F32, kind="ExternalInput").ap()
    hw["WgT"] = nc.dram_tensor("WgT", [D, MLP], BF16, kind="ExternalInput").ap()
    hw["WuT"] = nc.dram_tensor("WuT", [D, MLP], BF16, kind="ExternalInput").ap()
    hw["WdT"] = nc.dram_tensor("WdT", [MLP, D], BF16, kind="ExternalInput").ap()
    hw["identity"] = nc.dram_tensor("identity", [128, 128], F32, kind="ExternalInput").ap()
    hw["xT_out"] = nc.dram_tensor("xT_out", [D, T], F32, kind="ExternalOutput").ap()
    from contextlib import ExitStack
    with ExitStack() as stack:
        arh = stack.enter_context(nc.sbuf_tensor("arena", [128, 48000], I32))
        ps = [stack.enter_context(nc.psum_tensor("ps%d" % i, [128, 512], F32)) for i in range(8)]
        ps = [p.ap() for p in ps]
        ar = Arena(arh, 48000)
        base = alloc_base(ar, ps)
        b = alloc_l1ffn_bufs(ar, ps, base)

        def emit(ctx):
            s_o = ctx.sem("st_o")
            emit_base_init(ctx, b, hw)
            emit_l1ffn(ctx, b, hw)
            ctx.g.wait(ctx.sem("ve_b"), ctx.sem("ve_b").val)
            ctx.g.wait(s_o, s_o.val)
            for qq in range(4):
                ctx.g.dma_start(out=_cm(hw["xT_out"][qq * 512:(qq + 1) * 512, :]),
                                in_=_sb3(b["x"][:, qq * 4 * T:(qq + 1) * 4 * T], 4)).then_inc(s_o, 16)
            ctx.g.wait(s_o, s_o.val)

        build_program(nc, emit)
    return nc



_PROGS = {}

def _get_prog(name, builder, *a):
    if name not in _PROGS:
        _PROGS[name] = builder(*a)
    return _PROGS[name]


CFG_L0 = dict(silu=True, biases=False, gate_full=False, ln_gamma=False)
CFG_L1 = dict(silu=False, biases=True, gate_full=True, ln_gamma=True)


def kernel(**inputs):
    import numpy as np
    p = host_prep(inputs)
    ac = np.ascontiguousarray
    cores = list(range(NCORES))
    nc = _get_prog("fused", build_fused)
    in_maps = []
    for c in cores:
        kv = np.full((1, NCORES), 3.0e4, np.float32)
        for j in range(c):
            kv[0, j] = c - 1 - j
        m = dict(identity=p["identity"], kvec=kv,
                 WcT=p["WcT"], genes=p["genes"], WdT0=p["WdT"], WuT0=p["WuT"],
                 nwf=p["nwf"], temp=p["temp"], fscale=p["fscale"],
                 Wq0=p["Wq0"], Wk0=p["Wk0"], Wv0=p["Wv0"], Wo0=p["Wo0"],
                 Wg1=p["Wg1"], Wg2=p["Wg2"], nw0=p["nw0"], ld0=p["ld0"],
                 Wq1=p["Wq1"], Wk1=p["Wk1"], Wv1=p["Wv1"], Wg1a=p["Wg1a"], Wo1=p["Wo1"],
                 gamma1=p["gamma1"], bq1=p["bq1"], bk1=p["bk1"], bo1=p["bo1"],
                 bv1r=p["bv1r"], bg1r=p["bg1r"], gnw1=p["gnw1"], gnb1=p["gnb1"],
                 gWp=ac(p["gWp"][:, c * 2048:(c + 1) * 2048]),
                 gSc=ac(p["gSc"][:, c * 2048:(c + 1) * 2048]),
                 uWp=ac(p["uWp"][:, c * 2048:(c + 1) * 2048]),
                 uSc=ac(p["uSc"][:, c * 2048:(c + 1) * 2048]),
                 dWp=ac(p["dWp"][:, c * 256:(c + 1) * 256]),
                 dSc=ac(p["dSc"][:, c * 256:(c + 1) * 256]),
                 xT_in=ac(p["xT"][:, c * T:(c + 1) * T]))
        in_maps.append(m)
    res = run_bass_kernel_spmd(nc, in_maps, core_ids=cores)
    full = np.concatenate([res.results[c]["xT_out"] for c in cores], axis=1)
    out = full[INV_PERM_D].T.reshape(1, SEQ, D)
    return np.ascontiguousarray(out.astype(np.float32))

def alloc_dequant_bufs(ar, W=256):
    b = {"dq_w": W}
    b["pk"] = ar.carve([128, 2 * W], I32)
    b["sc"] = ar.carve([128, 2 * W], F32)
    b["dqv"] = tuple(ar.carve([128, W], I32 if i < 2 else F32) for i in range(6))
    b["dqg"] = b["dqv"]
    b["wlo"] = ar.carve([128, 2 * W], BF16)
    b["whi"] = ar.carve([128, 2 * W], BF16)
    return b


def build_fused(upto=99):
    nc = bass.Bass("TRN2", target_bir_lowering=False, debug=False)
    EI = "ExternalInput"
    dt_ = nc.dram_tensor
    hw = {}
    hw["xT_in"] = dt_("xT_in", [D, T], F32, kind=EI).ap()
    hw["identity"] = dt_("identity", [128, 128], F32, kind=EI).ap()
    hw["kvec"] = dt_("kvec", [1, NCORES], F32, kind=EI).ap()
    for n, shp in (("WcT", [D, G]), ("genes", [G, GD]), ("WdT0", [GD, D]), ("WuT0", [GD, D])):
        hw[n] = dt_(n, shp, F32, kind=EI).ap()
    hw["nwf"] = dt_("nwf", [128, DCH], F32, kind=EI).ap()
    hw["temp"] = dt_("temp", [1, 1], F32, kind=EI).ap()
    hw["fscale"] = dt_("fscale", [1, 1], F32, kind=EI).ap()
    for n in ("Wq0", "Wk0", "Wv0", "Wo0"):
        hw[n] = dt_(n, [D, D], F32, kind=EI).ap()
    hw["Wg1"] = dt_("Wg1", [D, 128], F32, kind=EI).ap()
    hw["Wg2"] = dt_("Wg2", [128, D], F32, kind=EI).ap()
    hw["nw0"] = dt_("nw0", [128, DCH], F32, kind=EI).ap()
    hw["ld0"] = dt_("ld0", [1, H], F32, kind=EI).ap()
    for n in ("Wq1", "Wk1", "Wv1", "Wg1a", "Wo1"):
        hw[n] = dt_(n, [D, D], F32, kind=EI).ap()
    hw["gamma1"] = dt_("gamma1", [1, H], F32, kind=EI).ap()
    for n in ("bq1", "bk1", "bo1"):
        hw[n] = dt_(n, [128, DCH], F32, kind=EI).ap()
    hw["bv1r"] = dt_("bv1r", [1, D], F32, kind=EI).ap()
    hw["bg1r"] = dt_("bg1r", [1, D], F32, kind=EI).ap()
    hw["gnw1"] = dt_("gnw1", [1, D], F32, kind=EI).ap()
    hw["gnb1"] = dt_("gnb1", [1, D], F32, kind=EI).ap()
    hw["gWp"] = dt_("gWp", [1024, 2048], I32, kind=EI).ap()
    hw["gSc"] = dt_("gSc", [64, 2048], F32, kind=EI).ap()
    hw["uWp"] = dt_("uWp", [1024, 2048], I32, kind=EI).ap()
    hw["uSc"] = dt_("uSc", [64, 2048], F32, kind=EI).ap()
    hw["dWp"] = dt_("dWp", [8192, 256], I32, kind=EI).ap()
    hw["dSc"] = dt_("dSc", [512, 256], F32, kind=EI).ap()
    hw["xT_out"] = dt_("xT_out", [D, T], F32, kind="ExternalOutput").ap()
    hw["wcT_bf"] = dt_("wcT_bf", [D, G], BF16).ap()
    hw["o_sc"] = dt_("o_sc", [T, D], F32).ap()
    hw["qT_sc"] = dt_("qT_sc", [D, T], BF16).ap()
    hw["gate_sc"] = dt_("gate_sc", [T, D], BF16).ap()
    hw["ret_sc"] = dt_("ret_sc", [T, D], F32).ap()
    hw["gq_p"] = dt_("gq_p", [2048, 2048], BF16).ap()
    hw["uq_p"] = dt_("uq_p", [2048, 2048], BF16).ap()
    hw["dq_p"] = dt_("dq_p", [16384, 256], BF16).ap()
    hw["WgT_ag"] = dt_("WgT_ag", [NCORES * 2048, 2048], BF16).ap()
    hw["WuT_ag"] = dt_("WuT_ag", [NCORES * 2048, 2048], BF16).ap()
    hw["WdT_ag"] = dt_("WdT_ag", [NCORES * 16384, 256], BF16).ap()
    hw["T_cc"] = dt_("T_cc", [128, H * 128], F32).ap()
    hw["Tall"] = dt_("Tall", [NCORES * 128, H * 128], F32).ap()
    hw["st_cc"] = dt_("st_cc", [128, 4 * H], F32).ap()
    hw["st_ag"] = dt_("st_ag", [NCORES * 128, 4 * H], F32).ap()

    hw_f0 = dict(hw, WdT=hw["WdT0"], WuT=hw["WuT0"])
    hw_a0 = dict(hw, Wq=hw["Wq0"], Wk=hw["Wk0"], Wv=hw["Wv0"], ldsrc=hw["ld0"],
                 o_out=hw["o_sc"], qT_out=hw["qT_sc"], T_out=hw["T_cc"])
    hw_af0 = dict(hw, ldsrc=hw["ld0"], o_in=hw["o_sc"], qT_in=hw["qT_sc"],
                  Wo=hw["Wo0"])
    hw_a1 = dict(hw, Wq=hw["Wq1"], Wk=hw["Wk1"], Wv=hw["Wv1"], ldsrc=hw["gamma1"],
                 o_out=hw["o_sc"], qT_out=hw["qT_sc"], T_out=hw["T_cc"],
                 gate_out=hw["gate_sc"], bq=hw["bq1"], bk=hw["bk1"],
                 bv_row=hw["bv1r"], bg_row=hw["bg1r"])
    hw_af1 = dict(hw, ldsrc=hw["gamma1"], o_in=hw["o_sc"], qT_in=hw["qT_sc"],
                  ret_out=hw["ret_sc"], st_out=hw["st_cc"], bo=hw["bo1"])
    hw_n1 = dict(hw, stats=hw["st_ag"], gnw=hw["gnw1"], gnb=hw["gnb1"],
                 Wo=hw["Wo1"], bo=hw["bo1"], ret_in=hw["ret_sc"], gate_in=hw["gate_sc"])

    def slab_fn(kind, key):
        if kind in ("g", "u"):
            nn = key
            r, loc = nn // 16, nn % 16
            t = hw["WgT_ag"] if kind == "g" else hw["WuT_ag"]
            return _cm(t[r * 2048:(r + 1) * 2048, loc * 128:(loc + 1) * 128])
        seg, h = key
        r, hh = h // 2, h % 2
        return _cm(hw["WdT_ag"][r * 16384 + seg * 2048: r * 16384 + (seg + 1) * 2048,
                                hh * 128:(hh + 1) * 128])

    from contextlib import ExitStack
    with ExitStack() as stack:
        arh = stack.enter_context(nc.sbuf_tensor("arena", [128, 49000], I32))
        ps = [stack.enter_context(nc.psum_tensor("ps%d" % i, [128, 512], F32)) for i in range(8)]
        ps = [p.ap() for p in ps]
        ar = Arena(arh, 49000)
        base = alloc_base(ar, ps)
        b_dq = alloc_dequant_bufs(ar, W=256)
        m0 = ar.mark()
        hiw = {}
        b_f0 = alloc_ffn0_bufs(ar, ps, base); hiw["f0"] = ar.mark()
        ar.release(m0)
        b_at = alloc_attn_bufs(ar, ps, base); hiw["at"] = ar.mark()
        ar.release(m0)
        b_af = alloc_finish_bufs(ar, ps, base); hiw["af"] = ar.mark()
        ar.release(m0)
        b_l1 = alloc_l1ffn_bufs(ar, ps, base); hiw["l1"] = ar.mark()
        ar.release(m0)
        b_n1 = alloc_l1norm_bufs(ar, ps, base); hiw["n1"] = ar.mark()
        ar.release(m0)
        import sys as _sys
        print("[fused] base:", m0, "phase high-water:", hiw, file=_sys.stderr)

        def emit(ctx):
            cc = ctx.sem("cc_a")
            ccb = ctx.sem("cc_b")
            s_o = ctx.sem("st_o")
            rg = [list(range(NCORES))]
            ctx.bg_tasks = dequant_task_list(ctx, b_dq, hw, cc)
            emit_base_init(ctx, base, hw)
            ctx.run_bg(8)
            if upto >= 1:
                emit_ffn0_prologue(ctx, b_f0, hw_f0, True)
                emit_ffn0(ctx, b_f0, hw_f0)
                ctx.barrier()
            if upto >= 2:
                emit_attn_tables(ctx, b_at, hw_a0, CFG_L0)
                emit_attn_local_phase(ctx, b_at, hw_a0, CFG_L0)
                ctx.g.collective_compute("AllGather", ALU.bypass, replica_groups=rg,
                                         ins=[hw["T_cc"][:, :].opt()], outs=[hw["Tall"][:, :].opt()]).then_inc(ccb, 1)
                ctx.barrier()
            if upto >= 3:
                cfgf0 = dict(CFG_L0, norm="grms")
                emit_attn_tables(ctx, b_af, hw_af0, dict(cfgf0, biases=False, finish=True))
                emit_attn_finish(ctx, b_af, hw_af0, cfgf0)
                ctx.barrier()
            if upto >= 4:
                emit_ffn0_prologue(ctx, b_f0, hw_f0, False)
                emit_ffn0(ctx, b_f0, hw_f0)
                ctx.barrier()
            if upto >= 5:
                ctx.run_bg(10 ** 9)
                ctx.q.wait(cc, 3)
                emit_l1ffn(ctx, b_l1, hw, slab_fn=slab_fn)
                ctx.barrier()
            if upto >= 6:
                emit_attn_tables(ctx, b_at, hw_a1, CFG_L1)
                emit_attn_local_phase(ctx, b_at, hw_a1, CFG_L1)
                ctx.g.collective_compute("AllGather", ALU.bypass, replica_groups=rg,
                                         ins=[hw["T_cc"][:, :].opt()], outs=[hw["Tall"][:, :].opt()]).then_inc(ccb, 1)
                ctx.barrier()
            if upto >= 7:
                cfgf1 = dict(CFG_L1, norm="ret")
                emit_attn_tables(ctx, b_af, hw_af1, dict(cfgf1, biases=False, finish=True))
                emit_attn_finish(ctx, b_af, hw_af1, cfgf1)
                ctx.g.wait(s_o, s_o.val)
                ctx.g.collective_compute("AllGather", ALU.bypass, replica_groups=rg,
                                         ins=[hw["st_cc"][:, :].opt()], outs=[hw["st_ag"][:, :].opt()]).then_inc(ccb, 1)
                ctx.barrier()
            if upto >= 8:
                ctx.q.wait(ccb, ccb.val)
                emit_l1_norm(ctx, b_n1, hw_n1)
                ctx.barrier()
            if upto >= 9:
                emit_l1ffn(ctx, b_l1, hw, slab_fn=slab_fn)
                ctx.barrier()
            ctx.g.wait(ctx.sem("ve_b"), ctx.sem("ve_b").val)
            ctx.g.wait(s_o, s_o.val)
            for qq in range(4):
                ctx.g.dma_start(out=_cm(hw["xT_out"][qq * 512:(qq + 1) * 512, :]),
                                in_=_sb3(base["x"][:, qq * 4 * T:(qq + 1) * 4 * T], 4)).then_inc(s_o, 16)
            ctx.g.wait(s_o, s_o.val)

        build_program(nc, emit)
    return nc
